# version 13
# baseline (speedup 1.0000x reference)
import contextlib
import numpy as np
import ml_dtypes
import concourse.bass as bass
import concourse.mybir as mybir
from concourse.bass_utils import run_bass_kernel_spmd

F32, BF16 = mybir.dt.float32, mybir.dt.bfloat16
AF = mybir.ActivationFunctionType
ALU = mybir.AluOpType
AX = mybir.AxisListType
BFNP = ml_dtypes.bfloat16

T, D = 2048, 1024
NCH, NTT, NQB = 8, 16, 4
NH, HD = 16, 64
NE, FE = 16, 512
QB, TTS = 512, 128
LAT_A = 256
DC, DR = 256, 128
EPS = 1e-5

ENGS = ("sync", "tensor", "vector", "scalar", "gpsimd")


class _Sem:
    __slots__ = ("name", "n", "h")

    def __init__(self, name):
        self.name, self.n, self.h = name, 0, None


class Rec:

    def __init__(self):
        self.ops = {e: [] for e in ENGS}
        self.sems = {}
        self.res = {}
        self.maxw = {e: {} for e in ENGS}
        self.pending = {}
        self.pend_list = {e: [] for e in ENGS}
        self.eng_ctr = {e: "c:" + e for e in ENGS}
        self.n_inst = {e: 0 for e in ENGS}
        self.wrote_at = {e: {} for e in ENGS}
        self.nops = 0

    def sem(self, name):
        s = self.sems.get(name)
        if s is None:
            s = self.sems[name] = _Sem(name)
        return s

    def wait_now(self, eng, semname, n):
        self._wait(eng, self.sem(semname), n)

    def _wait(self, eng, sem, n):
        if n <= 0 or sem.name == self.eng_ctr[eng]:
            return
        if self.maxw[eng].get(sem.name, 0) >= n:
            return
        self.maxw[eng][sem.name] = n
        self.ops[eng].append(("w", sem, n))

    def _dep(self, eng, r, need_readers):
        if r in self.pending and self.pending[r] != eng:
            raise RuntimeError(f"unresolved cross-engine dep on {r}")
        d = self.res.get(r)
        if not d:
            return
        if d["w"] is not None:
            self._wait(eng, *d["w"])
        if need_readers:
            for ev in d["r"]:
                self._wait(eng, *ev)

    def op(self, eng, fn, reads=(), writes=(), sem=None, inc=1):
        if eng in ("vector", "gpsimd", "scalar"):
            last = self.n_inst[eng] - 1
            hazard = any(self.wrote_at[eng].get(r, -2) == last for r in reads)
            if hazard and eng != "scalar":
                self.ops[eng].append(("i", lambda e: e.engine_nop(), None, 1))
                self.n_inst[eng] += 1
                self.nops += 1
        for r in reads:
            self._dep(eng, r, False)
        for r in writes:
            self._dep(eng, r, True)
        ev = None
        if sem is not None:
            s = self.sem(sem)
            s.n += inc
            ev = (s, s.n)
        self.ops[eng].append(("i", fn, ev, inc))
        for r in writes:
            self.wrote_at[eng][r] = self.n_inst[eng]
        self.n_inst[eng] += 1
        if ev is not None:
            for kind, r in self.pend_list[eng]:
                self._record(r, kind, ev)
                self.pending.pop(r, None)
            self.pend_list[eng].clear()
            for r in writes:
                self._record(r, "w", ev)
            for r in reads:
                self._record(r, "r", ev)
        else:
            for r in writes:
                self.pend_list[eng].append(("w", r))
                self.pending[r] = eng
            for r in reads:
                self.pend_list[eng].append(("r", r))
                self.pending[r] = eng

    def _record(self, r, kind, ev):
        d = self.res.setdefault(r, {"w": None, "r": []})
        if kind == "w":
            d["w"] = ev
            d["r"] = []
        else:
            d["r"].append(ev)

    def replay(self, nc, stack):
        for s in self.sems.values():
            s.h = stack.enter_context(nc.semaphore(s.name.replace(":", "_")))
        stack.enter_context(nc.allow_low_precision(
            reason="bf16 operand rounding is intentional throughout"))
        with nc.Block() as blk:
            def runner(name):
                def run(eng_obj):
                    for item in self.ops[name]:
                        if item[0] == "w":
                            eng_obj.wait_ge(item[1].h, item[2])
                        else:
                            inst = item[1](eng_obj)
                            if item[2] is not None:
                                inst.then_inc(item[2][0].h, item[3])
                return run
            blk.sync(runner("sync"))
            blk.tensor(runner("tensor"))
            blk.vector(runner("vector"))
            blk.scalar(runner("scalar"))
            blk.gpsimd(runner("gpsimd"))


class PV_:
    __slots__ = ("t", "c")

    def __init__(self, t, c):
        self.t, self.c = t, c

    def __getitem__(self, idx):
        if not isinstance(idx, tuple):
            idx = (idx, slice(None, None))
        pslice, fslice = idx
        p0 = pslice.start or 0
        p1 = 128 if pslice.stop is None else pslice.stop
        f0 = fslice.start or 0
        f1 = 512 if fslice.stop is None else fslice.stop
        return self.t[p0:p1, self.c + f0:self.c + f1]


GCOLS = 69504
RA = 0
RB = 16384
RC0 = 32768
RC1 = 41216
RD0 = 49408
RD1 = 53504
RE = 57600
RF = 61696
RH = 65792
RI = 66816
RROWF = 68480
VSTR = 1040
VH = 65


class Build:
    def __init__(self, stages=("attn_a", "ffn1", "attn_b", "ffn2"), taps=()):
        self.stages = stages
        self.taps = list(taps)
        self.nc = bass.Bass("TRN2", target_bir_lowering=False)
        self.rec = Rec()
        self.stack = contextlib.ExitStack()
        self._declare_io()
        self._declare_sbuf()
        self.tapnames = []
        self.out_done = set()

    def _declare_io(self):
        nc = self.nc
        dram = lambda n, sh, dt=F32, kind="ExternalInput": nc.dram_tensor(n, sh, dt, kind=kind)
        d = self.d = {}
        d["x"] = dram("x", [T, D])
        d["a_wq"] = dram("a_wq", [D, D])
        d["a_wka"] = dram("a_wka", [LAT_A, D])
        d["a_wkb"] = dram("a_wkb", [D, LAT_A])
        d["a_wva"] = dram("a_wva", [LAT_A, D])
        d["a_wvb"] = dram("a_wvb", [D, LAT_A])
        d["a_wo"] = dram("a_wo", [D, D])
        for n, sh in [("b_wq", [D, DC]), ("b_bq", [DC]), ("b_wqc", [DC, 896]),
                      ("b_bqc", [896]), ("b_wqr", [D, DR]), ("b_bqr", [DR]),
                      ("b_wkv", [D, DC]), ("b_bkv", [DC]), ("b_wkr", [D, DR]),
                      ("b_bkr", [DR]), ("b_wkc", [DC, 896]), ("b_bkc", [896]),
                      ("b_wv", [DC, D]), ("b_bv", [D]), ("b_wo", [D, D]),
                      ("b_bo", [D])]:
            d[n] = dram(n, sh)
        d["f_wg"] = dram("f_wg", [NE, D])
        d["f_w1"] = dram("f_w1", [NE, FE, D])
        d["f_w3"] = dram("f_w3", [NE, FE, D])
        d["f_w2"] = dram("f_w2", [NE, D, FE])
        d["f_w1s"] = dram("f_w1s", [D, D])
        d["f_w3s"] = dram("f_w3s", [D, D])
        d["f_w2s"] = dram("f_w2s", [D, D])
        d["identf"] = dram("identf", [128, 128])
        d["identb"] = dram("identb", [128, 128], BF16)
        d["onesb"] = dram("onesb", [1, 128], BF16)
        d["ones128"] = dram("ones128", [128, 1], BF16)
        d["onesAB"] = dram("onesAB", [128, 2], BF16)
        d["ropeAc"] = dram("ropeAc", [T, 128], BF16)
        d["ropeAs"] = dram("ropeAs", [T, 128], BF16)
        d["ropeBc"] = dram("ropeBc", [128, T], BF16)
        d["ropeBs"] = dram("ropeBs", [128, T], BF16)
        d["tri"] = dram("tri", [128, 128], BF16)
        d["y"] = dram("y", [T, D], F32, kind="ExternalOutput")
        scr = lambda n, sh: nc.dram_tensor(n, sh, BF16, kind="Internal")
        s = self.sd = {}
        s["a_wq"] = scr("s_a_wq", [D, D])
        s["a_wka"] = scr("s_a_wka", [LAT_A, D])
        s["a_wkb"] = scr("s_a_wkb", [D, LAT_A])
        s["a_wva"] = scr("s_a_wva", [LAT_A, D])
        s["a_wvb"] = scr("s_a_wvb", [D, LAT_A])
        s["a_wo"] = scr("s_a_wo", [D, D])
        s["f_wg"] = scr("s_f_wg", [NE, D])
        s["f_w1"] = scr("s_f_w1", [NE, FE, D])
        s["f_w3"] = scr("s_f_w3", [NE, FE, D])
        s["f_w2"] = scr("s_f_w2", [NE, D, FE])
        s["f_w1s"] = scr("s_f_w1s", [D, D])
        s["f_w3s"] = scr("s_f_w3s", [D, D])
        s["f_w2s"] = scr("s_f_w2s", [D, D])

    def _declare_sbuf(self):
        nc, st = self.nc, self.stack
        self.xres = st.enter_context(nc.sbuf_tensor("xres", [128, NCH * T], F32))
        self.G = st.enter_context(nc.sbuf_tensor("G", [128, GCOLS], BF16))
        self.Gf = self.G.bitcast(F32)
        self.smb = st.enter_context(nc.sbuf_tensor("smb", [1, 2048], BF16))
        self.epsf = st.enter_context(nc.sbuf_tensor("epsf", [1, 1], F32))
        self.negM = st.enter_context(nc.sbuf_tensor("negM", [128, 96], F32))
        self.biasf = st.enter_context(nc.sbuf_tensor("biasf", [128, 32], F32))
        self.cidentf = st.enter_context(nc.sbuf_tensor("cidentf", [128, 128], F32))
        self.cidentb = st.enter_context(nc.sbuf_tensor("cidentb", [128, 128], BF16))
        self.conesb = st.enter_context(nc.sbuf_tensor("conesb", [1, 128], BF16))
        self.cones128 = st.enter_context(nc.sbuf_tensor("cones128", [128, 1], BF16))
        self.conesAB = st.enter_context(nc.sbuf_tensor("conesAB", [128, 2], BF16))
        self.ctri = st.enter_context(nc.sbuf_tensor("ctri", [128, 128], BF16))
        ps = lambda n, sh, dt: st.enter_context(nc.psum_tensor(n, sh, dt))
        self.pSt = ps("pSt", [128, 2048], F32)
        self.p23 = ps("p23", [128, 1024], F32)
        self.p4t = ps("pb4", [128, 512], F32)
        self.pm = ps("pm", [128, 512], F32)
        self.pStb = self.pSt.bitcast(BF16)
        self.p = [PV_(self.pSt, 0), PV_(self.pSt, 512),
                  PV_(self.p23, 0), PV_(self.p23, 512), PV_(self.p4t, 0)]
        self.pmv = PV_(self.pm, 0)

    def g2(self, col, n, p0=0, pn=128):
        return bass.AP(self.G, p0 * GCOLS + col, [[GCOLS, pn], [1, n]])

    def g3(self, col, s1, c1, s2, c2):
        return bass.AP(self.G, col, [[GCOLS, 128], [s1, c1], [s2, c2]])

    def gf2(self, fcol, n, p0=0, pn=128):
        return bass.AP(self.Gf, p0 * (GCOLS // 2) + fcol, [[GCOLS // 2, pn], [1, n]])

    def gf3(self, fcol, s1, c1, s2, c2, pn=128):
        return bass.AP(self.Gf, fcol, [[GCOLS // 2, pn], [s1, c1], [s2, c2]])

    def row(self, fcol, n):
        return bass.AP(self.Gf, RROWF // 2 + fcol, [[GCOLS // 2, 1], [1, n]])

    def mm(self, out, lhsT, rhs, start, stop, reads=(), writes=(), sem=False):
        self.rec.op("tensor",
                    lambda t, o=out, a=lhsT, b=rhs, s1=start, s2=stop:
                    t.matmul(o, a, b, start=s1, stop=s2),
                    reads=reads, writes=writes,
                    sem=("c:tensor" if sem else None))

    def trans(self, out, in_, ident, reads=(), writes=(), sem=False):
        self.rec.op("tensor",
                    lambda t, o=out, i=in_, dd=ident: t.transpose(o, i, dd),
                    reads=reads, writes=writes,
                    sem=("c:tensor" if sem else None))

    def act(self, out, in_, func, bias=0.0, scale=1.0, reads=(), writes=(), sem=True):
        self.rec.op("scalar",
                    lambda s, o=out, i=in_, f=func, b=bias, sc=scale:
                    s.activation(o, i, f, bias=b, scale=sc),
                    reads=reads, writes=writes,
                    sem=("c:scalar" if sem else None))

    def vop(self, fn, reads=(), writes=(), sem=True):
        self.rec.op("vector", fn, reads=reads, writes=writes,
                    sem=("c:vector" if sem else None))

    def sop(self, fn, reads=(), writes=(), sem=True):
        self.rec.op("scalar", fn, reads=reads, writes=writes,
                    sem=("c:scalar" if sem else None))

    def dma(self, q, out, in_, transpose=False, reads=(), writes=(), slow=False):
        self.rec.op(q,
                    lambda e, o=out, i=in_, tr=transpose, sl=slow:
                    (e.dma_start(o, i, transpose=True) if tr else
                     e.dma_start(o, i, allow_slow_non_contiguous=sl)),
                    reads=reads, writes=writes, sem="d:" + q, inc=16)

    def emit_startup(self):
        d = self.dma
        for name, dst in [("identf", self.cidentf), ("identb", self.cidentb),
                          ("onesb", self.conesb), ("ones128", self.cones128),
                          ("onesAB", self.conesAB), ("tri", self.ctri)]:
            d("sync", dst[:], self.d[name].ap(), writes=("const:" + name,))
        self.vop(lambda v: v.memset(self.epsf[:], EPS), writes=("epsf",))
        for qq in range(4):
            srcq = bass.AP(self.d["x"], qq * 4 * TTS * D,
                           [[D, 128], [TTS * D, 4], [1, D]])
            dstq = bass.AP(self.Gf, qq * 4096, [[GCOLS // 2, 128], [1024, 4], [1, 1024]])
            self.dma("sync", dstq, srcq,
                     writes=(f"A{qq}",) if qq < 2 else (f"B{(qq - 2) * 2}",
                                                        f"B{(qq - 2) * 2 + 1}",))
        for tt in range(NTT):
            buf = tt * 1024
            for half in range(2):
                bank, bn = self.p[half], f"p{half}"
                for c4 in range(4):
                    c = half * 4 + c4
                    self.trans(bass.AP(self.pSt, half * 512 + c4 * 128,
                                       [[2048, 128], [1, 128]]),
                               self.gf2(buf + c * 128, 128), self.cidentf[:],
                               reads=("A0", "A1", "A2", "A3", "B0", "B1", "B2", "B3", "const:identf"), writes=(bn,),
                               sem=(c4 == 3))
                dst2 = bass.AP(self.xres, half * 4 * T + tt * TTS,
                               [[NCH * T, 128], [T, 4], [1, 128]])
                self.vop(lambda v, dd=dst2, bb=bank: v.tensor_copy(dd, bb[:]),
                         reads=(bn,), writes=(f"X{tt // 4}",))
        g = self.dma
        for n in ["a_wka", "a_wkb", "a_wva", "a_wvb", "a_wq", "a_wo", "f_wg"]:
            g("gpsimd", self.sd[n].ap(), self.d[n].ap(), writes=("scr:" + n,))
        for e in range(NE):
            for n in ("f_w1", "f_w3", "f_w2"):
                g("gpsimd", self.sd[n].ap()[e], self.d[n].ap()[e],
                  writes=(f"scr:{n}:{e}",))
        for n in ["f_w1s", "f_w3s", "f_w2s"]:
            g("gpsimd", self.sd[n].ap(), self.d[n].ap(), writes=("scr:" + n,))
        ap = bass.AP(self.G, RC0 + 64, [[GCOLS, 128], [VSTR, 16], [VH, 16]])
        self.vop(lambda v, a=ap: v.memset(a, 1.0),
                 writes=("C0", "C1", "C2", "C3"))

    def emit_ln(self, qb, dst_base, dst_stride, dst_res):
        pm, p4, p0, p1 = self.pm, self.p[4], self.p[0], self.p[1]
        xsrc = bass.AP(self.xres, qb * QB, [[NCH * T, 128], [T, NCH], [1, QB]])
        dst3 = self.g3(dst_base, dst_stride, NCH, 1, QB)
        self.sop(lambda s, dd=dst3, x=xsrc: s.copy(dd, x),
                 reads=(f"X{qb}",), writes=(dst_res,))
        xsq3 = self.g3(RE, QB, NCH, 1, QB)
        self.sop(lambda s, dd=xsq3, x=dst3: s.activation(dd, x, AF.Square),
                 reads=(dst_res,), writes=("E", "HA", "HB"))
        for c in range(NCH):
            self.mm(pm[0:1, 0:512], self.cones128[:],
                    self.g2(dst_base + c * dst_stride, QB),
                    start=(c == 0), stop=(c == NCH - 1),
                    reads=(dst_res, "const:ones128"), writes=("pm",),
                    sem=(c == NCH - 1))
        for c in range(NCH):
            self.mm(pm[32:33, 0:512], self.cones128[:], self.g2(RE + c * QB, QB),
                    start=(c == 0), stop=(c == NCH - 1),
                    reads=("E", "const:ones128"), writes=("pm",),
                    sem=(c == NCH - 1))
        rA = self.row(0, 512)
        smb = self.smb
        self.vop(lambda v: v.tensor_scalar_mul(smb[0:1, 0:512], pm[0:1, 0:512], 1.0 / D),
                 reads=("pm",), writes=("smb0",))
        self.vop(lambda v, r=rA: v.tensor_scalar_mul(r, pm[32:33, 0:512], 1.0 / D),
                 reads=("pm",), writes=("rows",))
        self.vop(lambda v: v.tensor_tensor(p4[0:1, 0:512], smb[0:1, 0:512],
                                           smb[0:1, 0:512], ALU.mult),
                 reads=("smb0",), writes=("p4",))
        self.vop(lambda v, a=rA: v.tensor_tensor(a, a, p4[0:1, 0:512], ALU.subtract),
                 reads=("rows", "p4"), writes=("rows",))
        self.sop(lambda s, a=rA: s.activation(p4[32:33, 0:512], a, AF.Sqrt,
                                              bias=self.epsf[:]),
                 reads=("rows", "epsf"), writes=("p4",))
        self.vop(lambda v, a=rA: v.reciprocal(a, p4[32:33, 0:512]),
                 reads=("p4",), writes=("rows",))
        self.vop(lambda v, a=rA: v.tensor_copy(smb[0:1, 512:1024], a),
                 reads=("rows",), writes=("smb1",))
        self.mm(p0[:], self.conesb[:], smb[0:1, 0:512], True, True,
                reads=("smb0", "const:onesb"), writes=("p0",), sem=True)
        self.mm(p1[:], self.conesb[:], smb[0:1, 512:1024], True, True,
                reads=("smb1", "const:onesb"), writes=("p1",), sem=True)
        for c in range(NCH):
            dsl = self.g2(dst_base + c * dst_stride, QB)
            self.vop(lambda v, dd=dsl: v.tensor_tensor(dd, dd, p0[:], ALU.subtract),
                     reads=(dst_res, "p0"), writes=(dst_res,), sem=False)
            self.vop(lambda v, dd=dsl: v.tensor_tensor(dd, dd, p1[:], ALU.mult),
                     reads=("p1",), writes=(dst_res,), sem=(c == NCH - 1))

    def emit_attn_a(self):
        g2, dma = self.g2, self.dma
        sd = self.sd
        for c in range(NCH):
            dma("sync", g2(RD0 + c * 256, 256),
                sd["a_wka"].ap()[0:LAT_A, c * 128:(c + 1) * 128], transpose=True,
                reads=("scr:a_wka",), writes=("D0",))
        for lc in range(2):
            dma("sync", g2(RD0 + 2048 + lc * 1024, 1024),
                sd["a_wkb"].ap()[0:D, lc * 128:(lc + 1) * 128], transpose=True,
                reads=("scr:a_wkb",), writes=("D0",))
        for c in range(NCH):
            dma("sync", g2(RD1 + c * 256, 256),
                sd["a_wva"].ap()[0:LAT_A, c * 128:(c + 1) * 128], transpose=True,
                reads=("scr:a_wva",), writes=("D1",))
        for lc in range(2):
            dma("sync", g2(RD1 + 2048 + lc * 1024, 1024),
                sd["a_wvb"].ap()[0:D, lc * 128:(lc + 1) * 128], transpose=True,
                reads=("scr:a_wvb",), writes=("D1",))
        self.emit_ln(0, RA, T, "A0")
        for qb in range(NQB):
            self._ropeA_slices(qb)
            self._lat_tok(qb, RD0, RD0 + 2048, "D0", dst="k")
            if qb + 1 < NQB:
                self.emit_ln(qb + 1, RA + (qb + 1) * QB, T, f"A{qb + 1}")
            self._lat_tok(qb, RD1, RD1 + 2048, "D1", dst="v")
        for half in range(2):
            base, wn = (RD0, "D0") if half == 0 else (RD1, "D1")
            for c in range(NCH):
                dma("sync", g2(base + c * 512, 512),
                    sd["a_wq"].ap()[half * 512:(half + 1) * 512,
                                    c * 128:(c + 1) * 128], transpose=True,
                    reads=("scr:a_wq",), writes=(wn,))
        for qb in range(NQB):
            self._ropeA_slices(qb)
            self._qtok(qb)
        self._bounds()
        for half in range(2):
            base, wn = (RD0, "D0") if half == 0 else (RD1, "D1")
            for c in range(NCH):
                dma("sync", g2(base + c * 512, 512),
                    sd["a_wo"].ap()[half * 512:(half + 1) * 512,
                                    c * 128:(c + 1) * 128], transpose=True,
                    reads=("scr:a_wo",), writes=(wn,))
        for qb in range(NQB):
            prev = None
            for h0 in range(0, NH, 2):
                sts = self._attend2(qb, h0, causal=False)
                if prev is not None:
                    self._norm_tail(prev[0])
                    self._norm_tail(prev[1])
                prev = sts
            self._norm_tail(prev[0])
            self._norm_tail(prev[1])
            self._wo(qb, bias_cols=None)

    def _ropeA_slices(self, qb):
        for name, off in (("ropeAc", 0), ("ropeAs", 512)):
            src = bass.AP(self.d[name], qb * 4 * 128 * 128,
                          [[128, 128], [128 * 128, 4], [1, 128]])
            dst = self.g3(RI + off, 128, 4, 1, 128)
            self.dma("sync", dst, src, writes=("I",))

    def _lat_tok(self, qb, wa, wb, wn, dst):
        g2 = self.g2
        for lc in range(2):
            bank, bn = self.p[2 + lc], f"p{2 + lc}"
            for c in range(NCH):
                self.mm(bank[:], g2(wa + c * 256 + lc * 128, 128),
                        g2(RA + c * T + qb * QB, QB),
                        start=(c == 0), stop=(c == NCH - 1),
                        reads=(wn, f"A{qb}"), writes=(bn,), sem=(c == NCH - 1))
            self.sop(lambda s, dd=g2(RH + lc * 512, 512), b=bank: s.copy(dd, b[:]),
                     reads=(bn,), writes=("H",))
        for ttl in range(4):
            tt = qb * 4 + ttl
            for half in range(2):
                bank, bn = self.p[half], f"p{half}"
                for lc in range(2):
                    self.mm(bank[:], g2(RH + lc * 512 + ttl * 128, 128),
                            g2(wb + lc * 1024 + half * 512, 512),
                            start=(lc == 0), stop=(lc == 1),
                            reads=("H", wn), writes=(bn,), sem=(lc == 1))
                if dst == "k":
                    self.sop(lambda s, dd=g2(RF + ttl * 1024 + half * 512, 512),
                             b=bank: s.copy(dd, b[:]), reads=(bn,), writes=("F",))
                else:
                    dap = bass.AP(self.G, RC0 + tt * VSTR + half * 8 * VH,
                                  [[GCOLS, 128], [VH, 8], [1, 64]])
                    self.sop(lambda s, dd=dap, b=bank: s.copy(dd, b[:]),
                             reads=(bn,), writes=(f"C{tt // 4}",))
        if dst == "k":
            self._ropeA_apply(qb)
            self._tok_to_feat(qb, RB, "B")

    def _qtok(self, qb):
        g2 = self.g2
        for ttl in range(4):
            for half in range(2):
                base, wn = (RD0, "D0") if half == 0 else (RD1, "D1")
                bank, bn = self.p[half], f"p{half}"
                for c in range(NCH):
                    self.mm(bank[:], g2(RA + c * T + qb * QB + ttl * 128, 128),
                            g2(base + c * 512, 512),
                            start=(c == 0), stop=(c == NCH - 1),
                            reads=(f"A{qb}", wn), writes=(bn,), sem=(c == NCH - 1))
                self.sop(lambda s, dd=g2(RF + ttl * 1024 + half * 512, 512),
                         b=bank: s.copy(dd, b[:]), reads=(bn,), writes=("F",))
        self._ropeA_apply(qb)
        self._tok_to_feat(qb, RA, "A")

    def _ropeA_apply(self, qb):
        G = self.G
        for ttl in range(4):
            qe = bass.AP(G, RF + ttl * 1024, [[GCOLS, 128], [64, 16], [2, 8]])
            qo = bass.AP(G, RF + ttl * 1024 + 1, [[GCOLS, 128], [64, 16], [2, 8]])
            ce = bass.AP(G, RI + ttl * 128, [[GCOLS, 128], [0, 16], [1, 8]])
            se = bass.AP(G, RI + 512 + ttl * 128, [[GCOLS, 128], [0, 16], [1, 8]])
            t1 = bass.AP(G, RI + 1024, [[GCOLS, 128], [8, 16], [1, 8]])
            t2 = bass.AP(G, RI + 1152, [[GCOLS, 128], [8, 16], [1, 8]])
            t3 = bass.AP(G, RI + 1280, [[GCOLS, 128], [8, 16], [1, 8]])
            t4 = bass.AP(G, RI + 1408, [[GCOLS, 128], [8, 16], [1, 8]])
            self.vop(lambda v, o=t1, a=qe, b=ce: v.tensor_tensor(o, a, b, ALU.mult),
                     reads=("F", "I"), writes=("I",), sem=False)
            self.vop(lambda v, o=t2, a=qo, b=se: v.tensor_tensor(o, a, b, ALU.mult),
                     reads=("F", "I"), writes=("I",), sem=False)
            self.vop(lambda v, o=t3, a=qe, b=se: v.tensor_tensor(o, a, b, ALU.mult),
                     reads=("F", "I"), writes=("I",), sem=False)
            self.vop(lambda v, o=t4, a=qo, b=ce: v.tensor_tensor(o, a, b, ALU.mult),
                     reads=("F", "I"), writes=("I",), sem=False)
            self.vop(lambda v, o=qe, a=t1, b=t2: v.tensor_tensor(o, a, b, ALU.subtract),
                     reads=("I",), writes=("F",), sem=False)
            self.vop(lambda v, o=qo, a=t3, b=t4: v.tensor_tensor(o, a, b, ALU.add),
                     reads=("I",), writes=("F",), sem=True)

    def _tok_to_feat(self, qb, dst_base, dreg):
        for ttl in range(4):
            tt = qb * 4 + ttl
            buf = ttl % 2
            bn = "p0" if buf == 0 else "p1"
            for c in range(NCH):
                self.trans(bass.AP(self.pStb, buf * 1024 + c * 128,
                                   [[4096, 128], [1, 128]]),
                           self.g2(RF + ttl * 1024 + c * 128, 128),
                           self.cidentb[:],
                           reads=("F", "const:identb"), writes=(bn,),
                           sem=(c == NCH - 1))
            dst = bass.AP(self.G, dst_base + tt * TTS,
                          [[GCOLS, 128], [T, NCH], [1, TTS]])
            src_ = bass.AP(self.pStb, buf * 1024, [[4096, 128], [1, 1024]])
            self.vop(lambda v, dd=dst, b=src_: v.tensor_copy(dd, b),
                     reads=(bn,), writes=(dreg + str(qb),))

    def _bounds(self):
        g2, p4 = self.g2, self.p[4]
        for reg, base, iscol in (("A", RA, True), ("B", RB, False)):
            regs = tuple(reg + str(i) for i in range(4))
            half = lambda c: (c % 2) * T
            self.sop(lambda s, dd=g2(RF + half(0), T), x=g2(base, T):
                     s.activation(dd, x, AF.Square),
                     reads=regs, writes=("F",))
            for c in range(NCH):
                if c + 1 < NCH:
                    self.sop(lambda s, dd=g2(RF + half(c + 1), T),
                             x=g2(base + (c + 1) * T, T):
                             s.activation(dd, x, AF.Square),
                             reads=regs, writes=("F",))
                for hh in range(2):
                    h = 2 * c + hh
                    for qq in range(NQB):
                        self.mm(p4[0:1, 0:512], self.conesAB[:, hh:hh + 1],
                                g2(RF + half(c) + qq * QB, QB), True, True,
                                reads=("F", "const:onesAB"), writes=("p4",),
                                sem=True)
                        col = (h * 4 + qq) if iscol else (128 + h * 4 + qq)
                        dstr = self.row(col, 1)
                        self.vop(lambda v, dd=dstr: v.tensor_reduce(
                            dd, p4[0:1, 0:512], AX.X, ALU.max),
                            reads=("p4",), writes=("rows",))
        kin = bass.AP(self.Gf, RROWF // 2 + 128, [[GCOLS // 2, 1], [4, 16], [1, 4]])
        self.vop(lambda v: v.tensor_reduce(self.row(256, 16), kin, AX.X, ALU.max),
                 reads=("rows",), writes=("rows",))
        kexp = bass.AP(self.Gf, RROWF // 2 + 256, [[GCOLS // 2, 1], [1, 16], [0, 4]])
        self.vop(lambda v: v.tensor_tensor(self.row(272, 64), self.row(0, 64),
                                           kexp, ALU.mult),
                 reads=("rows",), writes=("rows",))
        self.sop(lambda s: s.activation(self.row(336, 64), self.row(272, 64), AF.Sqrt),
                 reads=("rows",), writes=("rows",))
        self.vop(lambda v: v.tensor_scalar(self.row(272, 64), self.row(336, 64),
                                           -0.125, -0.25, ALU.mult, ALU.add),
                 reads=("rows",), writes=("rows",))
        i0 = bass.AP(self.Gf, RROWF // 2 + 272, [[GCOLS // 2, 1], [8, 8], [1, 4]])
        i1 = bass.AP(self.Gf, RROWF // 2 + 276, [[GCOLS // 2, 1], [8, 8], [1, 4]])
        self.vop(lambda v: v.tensor_tensor(self.row(336, 32), i0, i1, ALU.min),
                 reads=("rows",), writes=("rows",))
        self.vop(lambda v: v.tensor_copy(self.smb[0:1, 0:96], self.row(272, 96)),
                 reads=("rows",), writes=("smb0",))
        self.mm(self.pm[:, 0:96], self.conesb[:], self.smb[0:1, 0:96], True, True,
                reads=("smb0", "const:onesb"), writes=("pm",), sem=True)
        self.vop(lambda v: v.tensor_copy(self.negM[:], self.pm[:, 0:96]),
                 reads=("pm",), writes=("negM",))

    def _attend2(self, qb, h0, causal):
        g2, G = self.g2, self.G
        c = h0 // 2
        kmax = (4 * qb + 4) if causal else NTT
        pi = c & 1
        if pi == 0:
            obank, obn = (self.p[2], self.p[3]), ("p2", "p3")
        else:
            obank, obn = (self.p[4], self.pmv), ("p4", "pm")
        offs = (0, 64)
        bnames = (("p0", "p1"), ("s2", "s3"))
        qrhs = [G[o:o + 64, RA + c * T + qb * QB: RA + c * T + qb * QB + QB]
                for o in offs]

        def st2(kc, w):
            if kc >= kmax:
                return
            out = self.pSt[0:128, w * 1024 + (kc % 2) * 512:
                           w * 1024 + (kc % 2) * 512 + 512]
            o = offs[w]
            lhsT = G[o:o + 64, RB + c * T + kc * TTS: RB + c * T + (kc + 1) * TTS]
            self.mm(out, lhsT, qrhs[w], True, True,
                    reads=(f"A{qb}", f"B{kc // 4}"),
                    writes=(bnames[w][kc % 2],), sem=True)

        pbufs = ((RE, "HA"), (RE + 2048, "HB"))

        for w in range(2):
            st2(0, w)
            st2(1, w)
        for kcp in range(0, kmax, 2):
            pcol, pname = pbufs[(kcp // 2) % 2]
            for w in range(2):
                mcol = (h0 + w) * 4 + qb
                self.act(g2(pcol + w * 1024, 1024),
                         self.pSt[:, w * 1024:w * 1024 + 1024], AF.Exp,
                         bias=self.negM[:, mcol:mcol + 1], scale=0.125,
                         reads=bnames[w] + ("negM",), writes=(pname,))
                st2(kcp + 2, w)
                st2(kcp + 3, w)
            if causal:
                for w in range(2):
                    for j in range(2):
                        kc = kcp + j
                        if kc >= 4 * qb:
                            r = kc - 4 * qb
                            col = pcol + w * 1024 + j * 512
                            if r > 0:
                                self.vop(lambda v, a=g2(col, r * 128):
                                         v.memset(a, 0.0),
                                         reads=(pname,), writes=(pname,), sem=False)
                            self.vop(lambda v, a=g2(col + r * 128, 128):
                                     v.tensor_tensor(a, a, self.ctri[:], ALU.mult),
                                     reads=("const:tri", pname), writes=(pname,))
            for w in range(2):
                for j in range(2):
                    kc = kcp + j
                    vap = bass.AP(G, RC0 + kc * VSTR + (h0 + w) * VH,
                                  [[GCOLS, 128], [1, VH]])
                    self.mm(obank[w][0:VH, :], vap,
                            g2(pcol + w * 1024 + j * 512, 512),
                            start=(kc == 0), stop=(kc == kmax - 1),
                            reads=(f"C{kc // 4}", pname), writes=(obn[w],),
                            sem=(j == 1))
        outs = []
        for w in range(2):
            h = h0 + w
            off = offs[w]
            osl = G[off:off + 64, RF + c * QB: RF + c * QB + QB]
            self.vop(lambda v, o=osl, b=obank[w]: v.tensor_copy(o, b[0:64, :]),
                     reads=(obn[w],), writes=("F",))
            slot = pi * 2 + w
            smbsl = self.smb[0:1, slot * 512:slot * 512 + 512]
            self.vop(lambda v, r=smbsl, b=obank[w]: v.reciprocal(r, b[64:65, :]),
                     reads=(obn[w],), writes=(f"smb{slot}",))
            outs.append((qb, h, c, off, osl, slot, pi))
        return outs

    def _norm_tail(self, st):
        qb, h, c, off, osl, slot, pi = st
        w = 1 if off else 0
        smbsl = self.smb[0:1, slot * 512:slot * 512 + 512]
        if pi == 0:
            bank = self.p[2] if w == 0 else self.p[3]
            bn = "p2" if w == 0 else "p3"
            msrc = self.p23[off:off + 64, w * 512:w * 512 + 512]
        else:
            bank, bn = self.p[4], "p4"
            msrc = self.p4t[off:off + 64, :]
        self.mm(bank[:], self.conesb[:], smbsl, True, True,
                reads=(f"smb{slot}", "const:onesb"), writes=(bn,), sem=True)
        self.vop(lambda v, o=osl, m=msrc: v.tensor_tensor(o, o, m, ALU.mult),
                 reads=(bn, "F"), writes=("F",))

    def _wo(self, qb, bias_cols):
        for oc in range(NCH):
            half, base, wn = (0, RD0, "D0") if oc < 4 else (1, RD1, "D1")
            bank, bn = self.p[oc % 2], f"p{oc % 2}"
            for ic in range(NCH):
                lhsT = self.g2(base + ic * 512 + (oc % 4) * 128, 128)
                rhs = self.g2(RF + ic * QB, QB)
                self.mm(bank[:], lhsT, rhs, start=(ic == 0), stop=(ic == NCH - 1),
                        reads=(wn, "F"), writes=(bn,), sem=(ic == NCH - 1))
            xdst = self.xres[:, oc * T + qb * QB: oc * T + qb * QB + QB]
            if bias_cols is None:
                self.vop(lambda v, xd=xdst, bk=bank: v.tensor_add(xd, xd, bk[:]),
                         reads=(bn, f"X{qb}"), writes=(f"X{qb}",))
            else:
                b = self.biasf[:, bias_cols + oc: bias_cols + oc + 1]
                self.rec.op("vector",
                            lambda v, xd=xdst, bk=bank, bb=b:
                            v.scalar_tensor_tensor(xd, bk[:], bb, xd, ALU.add, ALU.add),
                            reads=(bn, f"X{qb}", "biasf"), writes=(f"X{qb}",),
                            sem="c:vector")

    def emit_ffn(self, final=False):
        g2 = self.g2
        for qb in range(NQB):
            self.emit_ln(qb, RA + qb * QB, T, f"A{qb}")
        for c in range(NCH):
            self.dma("sync", g2(RI + 1536 + c * 16, 16),
                     self.sd["f_wg"].ap()[0:NE, c * 128:(c + 1) * 128],
                     transpose=True, reads=("scr:f_wg",), writes=("I",))
        self._gate()
        slots = [(RD0, ("D0", "D1")), (RC1, ("C2", "C3")), (RC0, ("C0", "C1"))]
        for e in (0, 1):
            self._load_w13(e, *slots[e % 3])
            self._load_w2(e, e % 4)
        for pair in range(8):
            e0, e1 = 2 * pair, 2 * pair + 1
            if pair + 1 < 8:
                self._load_w2(e0 + 2, (e0 + 2) % 4)
                self._load_w2(e1 + 2, (e1 + 2) % 4)
                self._load_w13(e0 + 2, *slots[(e0 + 2) % 3])
            for tb in range(NQB):
                self._h_e_tb(e0, tb, *slots[e0 % 3], hoff=0)
                if tb == NQB - 1 and pair + 1 < 8:
                    self._load_w13(e1 + 2, *slots[(e1 + 2) % 3])
                self._h_e_tb(e1, tb, *slots[e1 % 3], hoff=2048)
                self._w2_pair(e0, tb)
        self._load_shared()
        self._h_phase_shared(final=final)

    def _load_w13(self, e, slot, names):
        for c in range(NCH):
            self.dma("sync", self.g2(slot + c * 512, 512),
                     self.sd["f_w1"].ap()[e][0:FE, c * 128:(c + 1) * 128],
                     transpose=True, reads=(f"scr:f_w1:{e}",), writes=names)
        for c in range(NCH):
            self.dma("sync", self.g2(slot + 4096 + c * 512, 512),
                     self.sd["f_w3"].ap()[e][0:FE, c * 128:(c + 1) * 128],
                     transpose=True, reads=(f"scr:f_w3:{e}",), writes=names)

    def _load_w2(self, e, qpos):
        for fc in range(4):
            self.dma("sync", self.g2(RB + qpos * 4096 + fc * 1024, 1024),
                     self.sd["f_w2"].ap()[e][0:D, fc * 128:(fc + 1) * 128],
                     transpose=True, reads=(f"scr:f_w2:{e}",),
                     writes=(f"B{qpos}",))

    def _load_shared(self):
        for c in range(NCH):
            self.dma("sync", self.g2(RC0 + c * 1024, 1024),
                     self.sd["f_w1s"].ap()[0:D, c * 128:(c + 1) * 128],
                     transpose=True, reads=("scr:f_w1s",),
                     writes=("C0", "C1"))
        for c in range(NCH):
            self.dma("sync", self.g2(RC1 + c * 1024, 1024),
                     self.sd["f_w3s"].ap()[0:D, c * 128:(c + 1) * 128],
                     transpose=True, reads=("scr:f_w3s",), writes=("C2", "C3"))
        for c in range(NCH):
            self.dma("sync", self.g2(RB + c * 1024, 1024),
                     self.sd["f_w2s"].ap()[0:D, c * 128:(c + 1) * 128],
                     transpose=True, reads=("scr:f_w2s",),
                     writes=("B0", "B1"))

    def _gate(self):
        g2, Gf = self.g2, self.Gf
        SGF = RF // 2
        for qb in range(NQB):
            for c in range(NCH):
                self.mm(self.pm[0:16, :], g2(RI + 1536 + c * 16, 16),
                        g2(RA + c * T + qb * QB, QB),
                        start=(c == 0), stop=(c == NCH - 1),
                        reads=("I", f"A{qb}"), writes=("pm",), sem=(c == NCH - 1))
            dstap = bass.AP(Gf, SGF + qb * QB, [[GCOLS // 2, 16], [1, QB]])
            self.sop(lambda s, dd=dstap: s.activation(dd, self.pm[0:16, :], AF.Sigmoid),
                     reads=("pm",), writes=("F",))
        EF = RE // 2
        for tt in range(NTT):
            src = bass.AP(Gf, SGF + tt * TTS, [[GCOLS // 2, 16], [1, TTS]])
            self.trans(self.p[4][0:128, 0:16], src, self.cidentf[0:16, 0:16],
                       reads=("F", "const:identf"), writes=("p4",), sem=True)
            dst = bass.AP(Gf, EF + tt * 16, [[GCOLS // 2, 128], [1, 16]])
            self.vop(lambda v, dd=dst, b=self.p[4]: v.tensor_copy(dd, b[0:128, 0:16]),
                     reads=("p4",), writes=("E",))
        SG, PW, GS, TH, MX, MRow = EF, EF + 256, EF + 640, EF + 704, EF + 720, EF + 768
        WS, RC = EF + 736, EF + 752
        e3 = lambda o, s1, c1, s2, c2: bass.AP(Gf, o, [[GCOLS // 2, 128],
                                                       [s1, c1], [s2, c2]])
        pairs = [(0, 1), (0, 2), (0, 3), (1, 2), (1, 3), (2, 3)]
        for p, (a, b) in enumerate(pairs):
            self.vop(lambda v, o=e3(PW + p * 64, 4, 16, 1, 4),
                     i0=e3(SG + a, 16, 16, 4, 4), i1=e3(SG + b, 16, 16, 4, 4):
                     v.tensor_tensor(o, i0, i1, ALU.add),
                     reads=("E",), writes=("E",), sem=(p == 5))
        self.vop(lambda v: v.tensor_reduce(
            self.gf2(GS, 64), bass.AP(Gf, PW, [[GCOLS // 2, 128], [1, 64], [64, 6]]),
            AX.X, ALU.max), reads=("E",), writes=("E",))
        for p, (a, b) in enumerate(pairs):
            self.vop(lambda v, o=self.gf2(PW + p * 16, 16),
                     i0=bass.AP(Gf, GS + a, [[GCOLS // 2, 128], [4, 16]]),
                     i1=bass.AP(Gf, GS + b, [[GCOLS // 2, 128], [4, 16]]):
                     v.tensor_tensor(o, i0, i1, ALU.min),
                     reads=("E",), writes=("E",), sem=(p == 5))
        self.vop(lambda v: v.tensor_reduce(
            self.gf2(TH, 16), bass.AP(Gf, PW, [[GCOLS // 2, 128], [1, 16], [16, 6]]),
            AX.X, ALU.max), reads=("E",), writes=("E",))
        gs3 = e3(GS, 4, 16, 1, 4)
        th3 = bass.AP(Gf, TH, [[GCOLS // 2, 128], [1, 16], [0, 4]])
        self.vop(lambda v: v.tensor_tensor(e3(GS, 4, 16, 1, 4), gs3, th3, ALU.is_ge),
                 reads=("E",), writes=("E",))
        self.vop(lambda v: v.tensor_scalar_mul(self.gf2(GS, 64), self.gf2(GS, 64),
                                               256.0),
                 reads=("E",), writes=("E",))
        for r in range(4):
            self.vop(lambda v, o=e3(MRow + r * 4, 16, 16, 4, 4)
                     if False else bass.AP(Gf, MRow + r, [[GCOLS // 2, 128],
                                                          [16, 16], [4, 4]]),
                     i0=bass.AP(Gf, SG + r, [[GCOLS // 2, 128], [16, 16], [4, 4]]),
                     i1=e3(GS, 4, 16, 1, 4):
                     v.tensor_tensor(o, i0, i1, ALU.add),
                     reads=("E",), writes=("E",), sem=(r == 3))
        mred = bass.AP(Gf, MRow, [[GCOLS // 2, 128], [16, 16], [1, 16]])
        mxx = bass.AP(Gf, MX, [[GCOLS // 2, 128], [1, 16], [0, 16]])
        for j in range(4):
            self.vop(lambda v: v.tensor_reduce(self.gf2(MX, 16), mred, AX.X, ALU.max),
                     reads=("E",), writes=("E",))
            if j < 3:
                self.vop(lambda v: v.tensor_tensor(self.gf2(PW, 256),
                                                   self.gf2(MRow, 256), mxx, ALU.is_ge),
                         reads=("E",), writes=("E",))
                self.vop(lambda v: v.tensor_scalar_mul(self.gf2(PW, 256),
                                                       self.gf2(PW, 256), 1024.0),
                         reads=("E",), writes=("E",))
                self.vop(lambda v: v.tensor_tensor(self.gf2(MRow, 256),
                                                   self.gf2(MRow, 256),
                                                   self.gf2(PW, 256), ALU.subtract),
                         reads=("E",), writes=("E",))
        self.vop(lambda v: v.tensor_tensor(self.gf2(PW, 256), self.gf2(MRow, 256),
                                           mxx, ALU.is_ge),
                 reads=("E",), writes=("E",))
        self.vop(lambda v: v.scalar_tensor_tensor(self.gf2(PW, 256),
                                                  self.gf2(MRow, 256), -500.0,
                                                  self.gf2(PW, 256),
                                                  ALU.is_lt, ALU.add),
                 reads=("E",), writes=("E",))
        self.vop(lambda v: v.tensor_tensor(self.gf2(PW, 256), self.gf2(PW, 256),
                                           self.gf2(SG, 256), ALU.mult),
                 reads=("E",), writes=("E",))
        wred = bass.AP(Gf, PW, [[GCOLS // 2, 128], [16, 16], [1, 16]])
        self.vop(lambda v: v.tensor_reduce(self.gf2(WS, 16), wred, AX.X, ALU.add),
                 reads=("E",), writes=("E",))
        self.vop(lambda v: v.tensor_scalar_add(self.gf2(WS, 16), self.gf2(WS, 16), 1e-6),
                 reads=("E",), writes=("E",))
        self.vop(lambda v: v.reciprocal(self.gf2(RC, 16), self.gf2(WS, 16)),
                 reads=("E",), writes=("E",))
        rcx = bass.AP(Gf, RC, [[GCOLS // 2, 128], [1, 16], [0, 16]])
        self.vop(lambda v: v.tensor_tensor(self.gf2(MRow, 256), self.gf2(PW, 256),
                                           rcx, ALU.mult),
                 reads=("E",), writes=("E",))
        self.vop(lambda v: v.tensor_copy(self.g2(RE + 3584, 256), self.gf2(MRow, 256)),
                 reads=("E",), writes=("E",))
        for tt in range(NTT):
            src = bass.AP(self.G, RE + 3584 + tt * 16, [[GCOLS, 128], [1, 16]])
            self.trans(bass.AP(self.pStb, 0, [[4096, 16], [1, 128]]), src,
                       self.cidentb[:],
                       reads=("E", "const:identb"), writes=("p0",), sem=True)
            dst = bass.AP(self.G, RE + 1024 + tt * TTS, [[GCOLS, 16], [1, TTS]])
            self.vop(lambda v, dd=dst: v.tensor_copy(dd, bass.AP(self.pStb, 0,
                                                     [[4096, 16], [1, TTS]])),
                     reads=("p0",), writes=("E",))

    def _h_e_tb(self, e, tb, slot, names, hoff):
        g2 = self.g2
        self.mm(self.pm[:], bass.AP(self.cidentb, e, [[128, 16], [0, 128]]),
                bass.AP(self.G, RE + 1024 + tb * QB, [[GCOLS, 16], [1, QB]]),
                True, True, reads=("E", "const:identb"), writes=("pm",), sem=True)
        for fc in range(4):
            gb, gn = self.p[2 * (fc % 2)], f"p{2 * (fc % 2)}"
            ub, un = self.p[2 * (fc % 2) + 1], f"p{2 * (fc % 2) + 1}"
            for c in range(NCH):
                self.mm(gb[:], g2(slot + c * 512 + fc * 128, 128),
                        g2(RA + c * T + tb * QB, QB),
                        start=(c == 0), stop=(c == NCH - 1),
                        reads=names + (f"A{tb}",), writes=(gn,), sem=(c == NCH - 1))
            for c in range(NCH):
                self.mm(ub[:], g2(slot + 4096 + c * 512 + fc * 128, 128),
                        g2(RA + c * T + tb * QB, QB),
                        start=(c == 0), stop=(c == NCH - 1),
                        reads=names + (f"A{tb}",), writes=(un,), sem=(c == NCH - 1))
            scol = RE + (fc % 2) * 512
            self.act(g2(scol, 512), gb[:], AF.Silu, reads=(gn,), writes=("E",))
            self.vop(lambda v, o=g2(scol, 512): v.tensor_tensor(
                o, o, self.pm[:], ALU.mult),
                reads=("E", "pm"), writes=("E",), sem=False)
            hdst = g2(RF + hoff + fc * 512, 512)
            self.vop(lambda v, o=hdst, s0=g2(scol, 512), u=ub:
                     v.tensor_tensor(o, s0, u[:], ALU.mult),
                     reads=("E", un), writes=("F",))

    def _w2_pair(self, e0, tb):
        g2 = self.g2
        q0, q1 = e0 % 4, (e0 + 1) % 4
        for hh in range(NCH):
            bank, bn = (self.p[4], "p4") if hh % 2 == 0 else (self.p[2], "p2")
            n = 0
            for eq, qq in ((0, q0), (1, q1)):
                for fc in range(4):
                    self.mm(bank[:], g2(RB + qq * 4096 + fc * 1024 + hh * 128, 128),
                            g2(RF + eq * 2048 + fc * 512, 512),
                            start=(n == 0), stop=(n == 7),
                            reads=(f"B{qq}", "F"), writes=(bn,), sem=(n == 7))
                    n += 1
            xdst = self.xres[:, hh * T + tb * QB: hh * T + tb * QB + QB]
            self.vop(lambda v, xd=xdst, bk=bank: v.tensor_add(xd, xd, bk[:]),
                     reads=(bn, f"X{tb}"), writes=(f"X{tb}",))

    def _h_phase_shared(self, final=False):
        g2 = self.g2
        for tb in range(NQB):
            for fc in range(NCH):
                gb, gn = self.p[2 * (fc % 2)], f"p{2 * (fc % 2)}"
                ub, un = self.p[2 * (fc % 2) + 1], f"p{2 * (fc % 2) + 1}"
                for c in range(NCH):
                    self.mm(gb[:], g2(RC0 + c * 1024 + fc * 128, 128),
                            g2(RA + c * T + tb * QB, QB),
                            start=(c == 0), stop=(c == NCH - 1),
                            reads=("C0", "C1", f"A{tb}"), writes=(gn,),
                            sem=(c == NCH - 1))
                for c in range(NCH):
                    self.mm(ub[:], g2(RC1 + c * 1024 + fc * 128, 128),
                            g2(RA + c * T + tb * QB, QB),
                            start=(c == 0), stop=(c == NCH - 1),
                            reads=("C2", "C3", f"A{tb}"), writes=(un,),
                            sem=(c == NCH - 1))
                scol = RE + (fc % 2) * 512
                self.act(g2(scol, 512), gb[:], AF.Silu, reads=(gn,), writes=("E",))
                hdst = g2(RF + fc * 512, 512)
                self.vop(lambda v, o=hdst, s0=g2(scol, 512), u=ub:
                         v.tensor_tensor(o, s0, u[:], ALU.mult),
                         reads=("E", un), writes=("F",))
            for hh in range(NCH):
                bank, bn = self.p[4 if hh % 2 == 0 else 2], f"p{4 if hh % 2 == 0 else 2}"
                for fc in range(NCH):
                    self.mm(bank[:], g2(RB + fc * 1024 + hh * 128, 128),
                            g2(RF + fc * 512, 512),
                            start=(fc == 0), stop=(fc == NCH - 1),
                            reads=("B0", "B1", "F"), writes=(bn,),
                            sem=(fc == NCH - 1))
                xdst = self.xres[:, hh * T + tb * QB: hh * T + tb * QB + QB]
                self.vop(lambda v, xd=xdst, bk=bank: v.tensor_add(xd, xd, bk[:]),
                         reads=(bn, f"X{tb}"), writes=(f"X{tb}",))
            if final:
                for tt in range(tb * 4, tb * 4 + 4):
                    self.emit_output_tt(tt)

    def _w2_phase_shared(self):
        pass

    def emit_attn_b(self):
        g2, dma = self.g2, self.dma
        d = self.d

        def nat_load(name, dst_col, in_dim, out_dim, wn):
            nchunk = in_dim // 128
            dst = self.g3(dst_col, out_dim, nchunk, 1, out_dim)
            src = bass.AP(d[name], 0, [[out_dim, 128], [128 * out_dim, nchunk],
                                       [1, out_dim]])
            dma("gpsimd", dst, src, writes=(wn,))

        def bias_load(name, col, n):
            dst = bass.AP(self.biasf, col, [[32, 128], [32, n]])
            dst = bass.AP(self.biasf, col, [[32, 128], [1, n]])
            src = bass.AP(d[name], 0, [[1, 128], [128, n]])
            dma("gpsimd", dst, src, writes=("biasf",), slow=True)

        onesap = bass.AP(self.G, RC0 + 64, [[GCOLS, 128], [VSTR, 16], [VH, 16]])
        self.vop(lambda v, a=onesap: v.memset(a, 1.0),
                 writes=("C0", "C1", "C2", "C3"))
        nat_load("b_wkv", RD0, D, DC, "D0")
        nat_load("b_wkc", RD0 + 2048, DC, 896, "D0")
        nat_load("b_wkr", RD1, D, DR, "D1")
        self._load_rotB("b_wkr", RD1 + 1024, "D1")
        nat_load("b_wv", RD1 + 2048, DC, D, "D1")
        bias_load("b_bkv", 0, 2)
        bias_load("b_bkc", 2, 7)
        bias_load("b_bkr", 9, 1)
        self._load_rotbias("b_bkr", 10)
        self.emit_ln(0, RA, T, "A0")
        for qb in range(NQB):
            self._ropeB_slices(qb)
            self._cproj(qb, RD0, "D0", 0, "H")
            self._cc_chunks(qb, RD0 + 2048, "D0", 2, RB, "B")
            if qb + 1 < NQB:
                self.emit_ln(qb + 1, RA + (qb + 1) * QB, T, f"A{qb + 1}")
            self._rot_chunk(qb, RD1, RD1 + 1024, "D1", 9, 10, RB, "B")
            self._vproj_b(qb)
        nat_load("b_wq", RD0, D, DC, "D0")
        nat_load("b_wqc", RD0 + 2048, DC, 896, "D0")
        nat_load("b_wqr", RD1, D, DR, "D1")
        self._load_rotB("b_wqr", RD1 + 1024, "D1")
        bias_load("b_bq", 11, 2)
        bias_load("b_bqc", 13, 7)
        bias_load("b_bqr", 20, 1)
        self._load_rotbias("b_bqr", 21)
        for qb in range(NQB):
            self._ropeB_slices(qb)
            self._cproj(qb, RD0, "D0", 11, "H")
            self._rot_chunk(qb, RD1, RD1 + 1024, "D1", 20, 21, RA, "A")
            self._cc_chunks(qb, RD0 + 2048, "D0", 13, RA, "A")
        self._bounds()
        for c in range(NCH):
            dstc = (RD0 + c * 1024) if c < 4 else (RD1 + (c - 4) * 1024)
            dst = self.g2(dstc, 1024)
            src = bass.AP(d["b_wo"], c * 128 * D, [[D, 128], [1, D]])
            dma("gpsimd", dst, src, writes=("D0",) if c < 4 else ("D1",))
        bias_load("b_bo", 22, 8)
        for qb in range(NQB):
            prev = None
            for h0 in range(0, NH, 2):
                sts = self._attend2(qb, h0, causal=True)
                if prev is not None:
                    self._norm_tail(prev[0])
                    self._norm_tail(prev[1])
                prev = sts
            self._norm_tail(prev[0])
            self._norm_tail(prev[1])
            self._wo_b(qb)

    def _load_rotB(self, name, dst_col, wn):
        d1 = self.g3(dst_col, 128, NCH, 1, 64)
        s1 = bass.AP(self.d[name], 64, [[DR, 128], [128 * DR, NCH], [1, 64]])
        self.dma("gpsimd", d1, s1, writes=(wn,))
        d2 = self.g3(dst_col + 64, 128, NCH, 1, 64)
        s2 = bass.AP(self.d[name], 0, [[DR, 128], [128 * DR, NCH], [1, 64]])
        self.dma("gpsimd", d2, s2, writes=(wn,))
        self.vop(lambda v, a=d1: v.tensor_scalar_mul(a, a, -1.0),
                 reads=(wn,), writes=(wn,))

    def _load_rotbias(self, name, col):
        dsth = bass.AP(self.biasf, col, [[32, 64], [1, 1]])
        dstl = bass.AP(self.biasf, 64 * 32 + col, [[32, 64], [1, 1]])
        self.dma("gpsimd", dsth, bass.AP(self.d[name], 64, [[1, 64], [1, 1]]),
                 writes=("biasf",))
        self.dma("gpsimd", dstl, bass.AP(self.d[name], 0, [[1, 64], [1, 1]]),
                 writes=("biasf",))
        self.vop(lambda v, a=dsth: v.tensor_scalar_mul(a, a, -1.0),
                 reads=("biasf",), writes=("biasf",))

    def _ropeB_slices(self, qb):
        for name, off in (("ropeBc", 0), ("ropeBs", 512)):
            src = bass.AP(self.d[name], qb * QB, [[T, 128], [1, QB]])
            self.dma("sync", self.g2(RI + off, QB), src, writes=("I",))

    def _cproj(self, qb, wcol, wn, bcol, dreg):
        for oc in range(2):
            bank, bn = self.p[2 + oc], f"p{2 + oc}"
            for c in range(NCH):
                self.mm(bank[:], self.g2(wcol + c * DC + oc * 128, 128),
                        self.g2(RA + c * T + qb * QB, QB),
                        start=(c == 0), stop=(c == NCH - 1),
                        reads=(wn, f"A{qb}"), writes=(bn,), sem=(c == NCH - 1))
            self.act(self.g2(RH + oc * 512, 512), bank[:], AF.Gelu,
                     bias=self.biasf[:, bcol + oc: bcol + oc + 1],
                     reads=(bn, "biasf"), writes=("H",))

    def _cc_chunks(self, qb, wcol, wn, bcol, dst_base, dreg):
        for oc in range(7):
            bank, bn = self.p[oc % 2], f"p{oc % 2}"
            for lc in range(2):
                self.mm(bank[:], self.g2(wcol + lc * 896 + oc * 128, 128),
                        self.g2(RH + lc * 512, 512),
                        start=(lc == 0), stop=(lc == 1),
                        reads=(wn, "H"), writes=(bn,), sem=(lc == 1))
            self.act(self.g2(dst_base + oc * T + qb * QB, QB), bank[:], AF.Gelu,
                     bias=self.biasf[:, bcol + oc: bcol + oc + 1],
                     reads=(bn, "biasf"), writes=(dreg + str(qb),))

    def _rot_chunk(self, qb, wcol, wBcol, wn, bcol, bBcol, dst_base, dreg):
        g2 = self.g2
        for which, wc, bc, fo in ((0, wcol, bcol, 0), (1, wBcol, bBcol, 512)):
            bank, bn = self.p[which], f"p{which}"
            for c in range(NCH):
                self.mm(bank[:], g2(wc + c * 128, 128),
                        g2(RA + c * T + qb * QB, QB),
                        start=(c == 0), stop=(c == NCH - 1),
                        reads=(wn, f"A{qb}"), writes=(bn,), sem=(c == NCH - 1))
            self.act(g2(RF + fo, 512), bank[:], AF.Identity,
                     bias=self.biasf[:, bc:bc + 1],
                     reads=(bn, "biasf"), writes=("F",))
        self.vop(lambda v: v.tensor_tensor(g2(RF + 1024, 512), g2(RF, 512),
                                           g2(RI, 512), ALU.mult),
                 reads=("F", "I"), writes=("F",), sem=False)
        self.vop(lambda v: v.tensor_tensor(g2(RF + 1536, 512), g2(RF + 512, 512),
                                           g2(RI + 512, 512), ALU.mult),
                 reads=("F", "I"), writes=("F",), sem=False)
        self.vop(lambda v: v.tensor_tensor(g2(RF + 1024, 512), g2(RF + 1024, 512),
                                           g2(RF + 1536, 512), ALU.add),
                 reads=("F",), writes=("F",), sem=True)
        self.act(g2(dst_base + 7 * T + qb * QB, QB), g2(RF + 1024, 512), AF.Gelu,
                 reads=("F",), writes=(dreg + str(qb),))

    def _vproj_b(self, qb):
        g2 = self.g2
        self.dma("gpsimd", self.smb[0:1, 0:1024],
                 bass.AP(self.d["b_bv"], 0, [[D, 1], [1, D]]),
                 writes=("smb0", "smb1"))
        for ttl in range(4):
            tt = qb * 4 + ttl
            for half in range(2):
                bank, bn = self.p[half], f"p{half}"
                for lc in range(2):
                    self.mm(bank[:], g2(RH + lc * 512 + ttl * 128, 128),
                            g2(RD1 + 2048 + lc * 1024 + half * 512, 512),
                            start=(lc == 0), stop=False,
                            reads=("H", "D1"), writes=(bn,), sem=False)
                self.mm(bank[:], self.conesb[:],
                        self.smb[0:1, half * 512:(half + 1) * 512],
                        start=False, stop=True,
                        reads=(f"smb{half}", "const:onesb"), writes=(bn,), sem=True)
                dap = bass.AP(self.G, RC0 + tt * VSTR + half * 8 * VH,
                              [[GCOLS, 128], [VH, 8], [1, 64]])
                self.sop(lambda s, dd=dap, b=bank: s.copy(dd, b[:]),
                         reads=(bn,), writes=(f"C{tt // 4}",))

    def _wo_b(self, qb):
        for oc in range(NCH):
            bank, bn = self.p[oc % 2], f"p{oc % 2}"
            for ic in range(NCH):
                wc = (RD0 + ic * 1024) if ic < 4 else (RD1 + (ic - 4) * 1024)
                wn = "D0" if ic < 4 else "D1"
                self.mm(bank[:], self.g2(wc + oc * 128, 128),
                        self.g2(RF + ic * QB, QB),
                        start=(ic == 0), stop=(ic == NCH - 1),
                        reads=(wn, "F"), writes=(bn,), sem=(ic == NCH - 1))
            xdst = self.xres[:, oc * T + qb * QB: oc * T + qb * QB + QB]
            b = self.biasf[:, 22 + oc: 22 + oc + 1]
            self.rec.op("vector",
                        lambda v, xd=xdst, bk=bank, bb=b:
                        v.scalar_tensor_tensor(xd, bk[:], bb, xd, ALU.add, ALU.add),
                        reads=(bn, f"X{qb}", "biasf"), writes=(f"X{qb}",),
                        sem="c:vector")

    def emit_output_tt(self, tt):
        buf = RD0 // 2 + (tt % 2) * 1024
        for half in range(2):
            bank, bn = self.p[half], f"p{half}"
            for c4 in range(4):
                c = half * 4 + c4
                self.trans(bass.AP(self.pSt, half * 512 + c4 * 128,
                                   [[2048, 128], [1, 128]]),
                           bass.AP(self.xres, c * T + tt * TTS,
                                   [[NCH * T, 128], [1, TTS]]),
                           self.cidentf[:],
                           reads=(f"X{tt // 4}", "const:identf"), writes=(bn,),
                           sem=(c4 == 3))
            dst = bass.AP(self.Gf, buf + half * 512, [[GCOLS // 2, 128], [1, 512]])
            self.vop(lambda v, dd=dst, b=bank: v.tensor_copy(dd, b[:]),
                     reads=(bn,), writes=(f"OUT{tt % 2}",))
        srcap = self.gf2(buf, 1024)
        dstd = bass.AP(self.d["y"], tt * TTS * D, [[D, 128], [1, D]])
        self.dma("gpsimd", dstd, srcap, reads=(f"OUT{tt % 2}",), writes=())
        self.out_done.add(tt)

    def emit_output(self):
        for tt in range(NTT):
            if tt not in self.out_done:
                self.emit_output_tt(tt)
        total = self.rec.sem("d:gpsimd").n
        self.rec.wait_now("gpsimd", "d:gpsimd", total)

    def tap(self, name):
        t = self.nc.dram_tensor("tap_" + name, [128, NCH * T], F32,
                                kind="ExternalOutput")
        self.tapnames.append("tap_" + name)
        self.dma("gpsimd", t.ap(), self.xres[:], reads=("X0", "X1", "X2", "X3"), writes=())

    def tapg(self, name, col, n, reads=()):
        t = self.nc.dram_tensor("tap_" + name, [128, n], F32, kind="ExternalOutput")
        self.tapnames.append("tap_" + name)
        self.dma("gpsimd", t.ap(), self.g2(col, n), reads=tuple(reads), writes=())

    def build(self):
        self.emit_startup()
        if "attn_a" in self.stages:
            self.emit_attn_a()
            if "attn_a" in self.taps:
                self.tap("attn_a")
        if "ffn1" in self.stages:
            self.emit_ffn()
            if "ffn1" in self.taps:
                self.tap("ffn1")
        if "attn_b" in self.stages:
            self.emit_attn_b()
            if "attn_b" in self.taps:
                self.tap("attn_b")
        if "ffn2" in self.stages:
            self.emit_ffn(final=True)
        self.emit_output()
        self.rec.replay(self.nc, self.stack)
        return self.nc


def make_consts():
    c = {}
    c["identf"] = np.eye(128, dtype=np.float32)
    c["identb"] = np.eye(128, dtype=np.float32).astype(BFNP)
    c["onesb"] = np.ones((1, 128), BFNP)
    c["ones128"] = np.ones((128, 1), BFNP)
    ab = np.zeros((128, 2), np.float32)
    ab[:64, 0] = 1.0
    ab[64:, 1] = 1.0
    c["onesAB"] = ab.astype(BFNP)
    inv = (10000.0 ** (-np.arange(8) * 2.0 / 64.0)).astype(np.float64)
    angh = np.arange(16)[:, None] * inv[None, :]
    rowc = np.cos(angh).reshape(1, 128)
    rows = np.sin(angh).reshape(1, 128)
    c["ropeAc"] = np.broadcast_to(rowc, (T, 128)).astype(BFNP)
    c["ropeAs"] = np.broadcast_to(rows, (T, 128)).astype(BFNP)
    invb = (10000.0 ** (-np.arange(64) * 2.0 / 128.0)).astype(np.float64)
    angb = np.arange(T)[None, :] * invb[:, None]
    c["ropeBc"] = np.concatenate([np.cos(angb), np.cos(angb)], 0).astype(BFNP)
    c["ropeBs"] = np.concatenate([np.sin(angb), np.sin(angb)], 0).astype(BFNP)
    kk = np.arange(128)[:, None]
    qq = np.arange(128)[None, :]
    c["tri"] = (qq >= kk).astype(np.float32).astype(BFNP)
    return c


_IN_MAP_NAMES = {
    "x": None,
    "a_wq": "L0_attn_a_wq", "a_wka": "L0_attn_a_wk_a", "a_wkb": "L0_attn_a_wk_b",
    "a_wva": "L0_attn_a_wv_a", "a_wvb": "L0_attn_a_wv_b", "a_wo": "L0_attn_a_wo",
    "b_wq": "L0_attn_b_wq", "b_bq": "L0_attn_b_bq", "b_wqc": "L0_attn_b_wqc",
    "b_bqc": "L0_attn_b_bqc", "b_wqr": "L0_attn_b_wqr", "b_bqr": "L0_attn_b_bqr",
    "b_wkv": "L0_attn_b_wkv", "b_bkv": "L0_attn_b_bkv", "b_wkr": "L0_attn_b_wkr",
    "b_bkr": "L0_attn_b_bkr", "b_wkc": "L0_attn_b_wkc", "b_bkc": "L0_attn_b_bkc",
    "b_wv": "L0_attn_b_wv", "b_bv": "L0_attn_b_bv", "b_wo": "L0_attn_b_wo",
    "b_bo": "L0_attn_b_bo",
    "f_wg": "L0_ffn_w_gate", "f_w1": "L0_ffn_w1_e", "f_w3": "L0_ffn_w3_e",
    "f_w2": "L0_ffn_w2_e", "f_w1s": "L0_ffn_w1_s", "f_w3s": "L0_ffn_w3_s",
    "f_w2s": "L0_ffn_w2_s",
}


def make_in_maps(inputs, taps_consts=None):
    consts = taps_consts or make_consts()
    base = {}
    for k, refname in _IN_MAP_NAMES.items():
        if refname is not None:
            base[k] = np.ascontiguousarray(np.asarray(inputs[refname], np.float32))
    base.update(consts)
    x = np.asarray(inputs["L0_attn_a_x"], np.float32)
    in_maps = []
    for i in range(8):
        m = dict(base)
        m["x"] = np.ascontiguousarray(x[i])
        in_maps.append(m)
    return in_maps


_NC_CACHE = {}


def get_nc(stages=("attn_a", "ffn1", "attn_b", "ffn2"), taps=()):
    key = (tuple(stages), tuple(taps))
    if key not in _NC_CACHE:
        b = Build(stages=stages, taps=taps)
        _NC_CACHE[key] = (b.build(), b.tapnames)
    return _NC_CACHE[key]


def kernel(**inputs):
    nc, _ = get_nc()
    in_maps = make_in_maps(inputs)
    res = run_bass_kernel_spmd(nc, in_maps, core_ids=list(range(8)))
    return np.stack([r["y"] for r in res.results], axis=0)


# version 14 (final)
# speedup vs baseline: 1.0235x; 1.0235x over previous
import contextlib
import numpy as np
import ml_dtypes
import concourse.bass as bass
import concourse.mybir as mybir
from concourse.bass_utils import run_bass_kernel_spmd

F32, BF16 = mybir.dt.float32, mybir.dt.bfloat16
AF = mybir.ActivationFunctionType
ALU = mybir.AluOpType
AX = mybir.AxisListType
BFNP = ml_dtypes.bfloat16

T, D = 2048, 1024
NCH, NTT, NQB = 8, 16, 4
NH, HD = 16, 64
NE, FE = 16, 512
QB, TTS = 512, 128
LAT_A = 256
DC, DR = 256, 128
EPS = 1e-5

ENGS = ("sync", "tensor", "vector", "scalar", "gpsimd")


class _Sem:
    __slots__ = ("name", "n", "h")

    def __init__(self, name):
        self.name, self.n, self.h = name, 0, None


class Rec:

    def __init__(self):
        self.ops = {e: [] for e in ENGS}
        self.sems = {}
        self.res = {}
        self.maxw = {e: {} for e in ENGS}
        self.pending = {}
        self.pend_list = {e: [] for e in ENGS}
        self.eng_ctr = {e: "c:" + e for e in ENGS}
        self.n_inst = {e: 0 for e in ENGS}
        self.wrote_at = {e: {} for e in ENGS}
        self.nops = 0

    def sem(self, name):
        s = self.sems.get(name)
        if s is None:
            s = self.sems[name] = _Sem(name)
        return s

    def wait_now(self, eng, semname, n):
        self._wait(eng, self.sem(semname), n)

    def _wait(self, eng, sem, n):
        if n <= 0 or sem.name == self.eng_ctr[eng]:
            return
        if self.maxw[eng].get(sem.name, 0) >= n:
            return
        self.maxw[eng][sem.name] = n
        self.ops[eng].append(("w", sem, n))

    def _dep(self, eng, r, need_readers):
        if r in self.pending and self.pending[r] != eng:
            raise RuntimeError(f"unresolved cross-engine dep on {r}")
        d = self.res.get(r)
        if not d:
            return
        if d["w"] is not None:
            self._wait(eng, *d["w"])
        if need_readers:
            for ev in d["r"]:
                self._wait(eng, *ev)

    def op(self, eng, fn, reads=(), writes=(), sem=None, inc=1):
        if eng in ("vector", "gpsimd", "scalar"):
            last = self.n_inst[eng] - 1
            hazard = any(self.wrote_at[eng].get(r, -2) == last for r in reads)
            if hazard and eng != "scalar":
                self.ops[eng].append(("i", lambda e: e.engine_nop(), None, 1))
                self.n_inst[eng] += 1
                self.nops += 1
        for r in reads:
            self._dep(eng, r, False)
        for r in writes:
            self._dep(eng, r, True)
        ev = None
        if sem is not None:
            s = self.sem(sem)
            s.n += inc
            ev = (s, s.n)
        self.ops[eng].append(("i", fn, ev, inc))
        for r in writes:
            self.wrote_at[eng][r] = self.n_inst[eng]
        self.n_inst[eng] += 1
        if ev is not None:
            for kind, r in self.pend_list[eng]:
                self._record(r, kind, ev)
                self.pending.pop(r, None)
            self.pend_list[eng].clear()
            for r in writes:
                self._record(r, "w", ev)
            for r in reads:
                self._record(r, "r", ev)
        else:
            for r in writes:
                self.pend_list[eng].append(("w", r))
                self.pending[r] = eng
            for r in reads:
                self.pend_list[eng].append(("r", r))
                self.pending[r] = eng

    def _record(self, r, kind, ev):
        d = self.res.setdefault(r, {"w": None, "r": []})
        if kind == "w":
            d["w"] = ev
            d["r"] = []
        else:
            d["r"].append(ev)

    def replay(self, nc, stack):
        for s in self.sems.values():
            s.h = stack.enter_context(nc.semaphore(s.name.replace(":", "_")))
        stack.enter_context(nc.allow_low_precision(
            reason="bf16 operand rounding is intentional throughout"))
        with nc.Block() as blk:
            def runner(name):
                def run(eng_obj):
                    for item in self.ops[name]:
                        if item[0] == "w":
                            eng_obj.wait_ge(item[1].h, item[2])
                        else:
                            inst = item[1](eng_obj)
                            if item[2] is not None:
                                inst.then_inc(item[2][0].h, item[3])
                return run
            blk.sync(runner("sync"))
            blk.tensor(runner("tensor"))
            blk.vector(runner("vector"))
            blk.scalar(runner("scalar"))
            blk.gpsimd(runner("gpsimd"))


class PV_:
    __slots__ = ("t", "c")

    def __init__(self, t, c):
        self.t, self.c = t, c

    def __getitem__(self, idx):
        if not isinstance(idx, tuple):
            idx = (idx, slice(None, None))
        pslice, fslice = idx
        p0 = pslice.start or 0
        p1 = 128 if pslice.stop is None else pslice.stop
        f0 = fslice.start or 0
        f1 = 512 if fslice.stop is None else fslice.stop
        return self.t[p0:p1, self.c + f0:self.c + f1]


GCOLS = 69504
RA = 0
RB = 16384
RC0 = 32768
RC1 = 41216
RD0 = 49408
RD1 = 53504
RE = 57600
RF = 61696
RH = 65792
RI = 66816
RROWF = 68480
VSTR = 1040
VH = 65


class Build:
    def __init__(self, stages=("attn_a", "ffn1", "attn_b", "ffn2"), taps=()):
        self.stages = stages
        self.taps = list(taps)
        self.nc = bass.Bass("TRN2", target_bir_lowering=False)
        self.rec = Rec()
        self.stack = contextlib.ExitStack()
        self._declare_io()
        self._declare_sbuf()
        self.tapnames = []
        self.out_done = set()

    def _declare_io(self):
        nc = self.nc
        dram = lambda n, sh, dt=F32, kind="ExternalInput": nc.dram_tensor(n, sh, dt, kind=kind)
        d = self.d = {}
        d["x"] = dram("x", [T, D])
        d["a_wq"] = dram("a_wq", [D, D])
        d["a_wka"] = dram("a_wka", [LAT_A, D])
        d["a_wkb"] = dram("a_wkb", [D, LAT_A])
        d["a_wva"] = dram("a_wva", [LAT_A, D])
        d["a_wvb"] = dram("a_wvb", [D, LAT_A])
        d["a_wo"] = dram("a_wo", [D, D])
        for n, sh in [("b_wq", [D, DC]), ("b_bq", [DC]), ("b_wqc", [DC, 896]),
                      ("b_bqc", [896]), ("b_wqr", [D, DR]), ("b_bqr", [DR]),
                      ("b_wkv", [D, DC]), ("b_bkv", [DC]), ("b_wkr", [D, DR]),
                      ("b_bkr", [DR]), ("b_wkc", [DC, 896]), ("b_bkc", [896]),
                      ("b_wv", [DC, D]), ("b_bv", [D]), ("b_wo", [D, D]),
                      ("b_bo", [D])]:
            d[n] = dram(n, sh)
        d["f_wg"] = dram("f_wg", [NE, D])
        d["f_w1"] = dram("f_w1", [NE, FE, D])
        d["f_w3"] = dram("f_w3", [NE, FE, D])
        d["f_w2"] = dram("f_w2", [NE, D, FE])
        d["f_w1s"] = dram("f_w1s", [D, D])
        d["f_w3s"] = dram("f_w3s", [D, D])
        d["f_w2s"] = dram("f_w2s", [D, D])
        d["identf"] = dram("identf", [128, 128])
        d["identb"] = dram("identb", [128, 128], BF16)
        d["onesb"] = dram("onesb", [1, 128], BF16)
        d["ones128"] = dram("ones128", [128, 1], BF16)
        d["onesAB"] = dram("onesAB", [128, 2], BF16)
        d["ropeAc"] = dram("ropeAc", [T, 128], BF16)
        d["ropeAs"] = dram("ropeAs", [T, 128], BF16)
        d["ropeBc"] = dram("ropeBc", [128, T], BF16)
        d["ropeBs"] = dram("ropeBs", [128, T], BF16)
        d["tri"] = dram("tri", [128, 128], BF16)
        d["y"] = dram("y", [T, D], F32, kind="ExternalOutput")
        scr = lambda n, sh: nc.dram_tensor(n, sh, BF16, kind="Internal")
        s = self.sd = {}
        s["a_wq"] = scr("s_a_wq", [D, D])
        s["a_wka"] = scr("s_a_wka", [LAT_A, D])
        s["a_wkb"] = scr("s_a_wkb", [D, LAT_A])
        s["a_wva"] = scr("s_a_wva", [LAT_A, D])
        s["a_wvb"] = scr("s_a_wvb", [D, LAT_A])
        s["a_wo"] = scr("s_a_wo", [D, D])
        s["f_wg"] = scr("s_f_wg", [NE, D])
        s["f_w1"] = scr("s_f_w1", [NE, FE, D])
        s["f_w3"] = scr("s_f_w3", [NE, FE, D])
        s["f_w2"] = scr("s_f_w2", [NE, D, FE])
        s["f_w1s"] = scr("s_f_w1s", [D, D])
        s["f_w3s"] = scr("s_f_w3s", [D, D])
        s["f_w2s"] = scr("s_f_w2s", [D, D])

    def _declare_sbuf(self):
        nc, st = self.nc, self.stack
        self.xres = st.enter_context(nc.sbuf_tensor("xres", [128, NCH * T], F32))
        self.G = st.enter_context(nc.sbuf_tensor("G", [128, GCOLS], BF16))
        self.Gf = self.G.bitcast(F32)
        self.smb = st.enter_context(nc.sbuf_tensor("smb", [1, 2048], BF16))
        self.epsf = st.enter_context(nc.sbuf_tensor("epsf", [1, 1], F32))
        self.negM = st.enter_context(nc.sbuf_tensor("negM", [128, 96], F32))
        self.biasf = st.enter_context(nc.sbuf_tensor("biasf", [128, 32], F32))
        self.cidentf = st.enter_context(nc.sbuf_tensor("cidentf", [128, 128], F32))
        self.cidentb = st.enter_context(nc.sbuf_tensor("cidentb", [128, 128], BF16))
        self.conesb = st.enter_context(nc.sbuf_tensor("conesb", [1, 128], BF16))
        self.cones128 = st.enter_context(nc.sbuf_tensor("cones128", [128, 1], BF16))
        self.conesAB = st.enter_context(nc.sbuf_tensor("conesAB", [128, 2], BF16))
        self.ctri = st.enter_context(nc.sbuf_tensor("ctri", [128, 128], BF16))
        ps = lambda n, sh, dt: st.enter_context(nc.psum_tensor(n, sh, dt))
        self.pSt = ps("pSt", [128, 2048], F32)
        self.p23 = ps("p23", [128, 1024], F32)
        self.p4t = ps("pb4", [128, 512], F32)
        self.pm = ps("pm", [128, 512], F32)
        self.pStb = self.pSt.bitcast(BF16)
        self.p = [PV_(self.pSt, 0), PV_(self.pSt, 512),
                  PV_(self.p23, 0), PV_(self.p23, 512), PV_(self.p4t, 0)]
        self.pmv = PV_(self.pm, 0)

    def g2(self, col, n, p0=0, pn=128):
        return bass.AP(self.G, p0 * GCOLS + col, [[GCOLS, pn], [1, n]])

    def g3(self, col, s1, c1, s2, c2):
        return bass.AP(self.G, col, [[GCOLS, 128], [s1, c1], [s2, c2]])

    def gf2(self, fcol, n, p0=0, pn=128):
        return bass.AP(self.Gf, p0 * (GCOLS // 2) + fcol, [[GCOLS // 2, pn], [1, n]])

    def gf3(self, fcol, s1, c1, s2, c2, pn=128):
        return bass.AP(self.Gf, fcol, [[GCOLS // 2, pn], [s1, c1], [s2, c2]])

    def row(self, fcol, n):
        return bass.AP(self.Gf, RROWF // 2 + fcol, [[GCOLS // 2, 1], [1, n]])

    def mm(self, out, lhsT, rhs, start, stop, reads=(), writes=(), sem=False):
        self.rec.op("tensor",
                    lambda t, o=out, a=lhsT, b=rhs, s1=start, s2=stop:
                    t.matmul(o, a, b, start=s1, stop=s2),
                    reads=reads, writes=writes,
                    sem=("c:tensor" if sem else None))

    def trans(self, out, in_, ident, reads=(), writes=(), sem=False):
        self.rec.op("tensor",
                    lambda t, o=out, i=in_, dd=ident: t.transpose(o, i, dd),
                    reads=reads, writes=writes,
                    sem=("c:tensor" if sem else None))

    def act(self, out, in_, func, bias=0.0, scale=1.0, reads=(), writes=(), sem=True):
        self.rec.op("scalar",
                    lambda s, o=out, i=in_, f=func, b=bias, sc=scale:
                    s.activation(o, i, f, bias=b, scale=sc),
                    reads=reads, writes=writes,
                    sem=("c:scalar" if sem else None))

    def vop(self, fn, reads=(), writes=(), sem=True):
        self.rec.op("vector", fn, reads=reads, writes=writes,
                    sem=("c:vector" if sem else None))

    def sop(self, fn, reads=(), writes=(), sem=True):
        self.rec.op("scalar", fn, reads=reads, writes=writes,
                    sem=("c:scalar" if sem else None))

    def dma(self, q, out, in_, transpose=False, reads=(), writes=(), slow=False):
        self.rec.op(q,
                    lambda e, o=out, i=in_, tr=transpose, sl=slow:
                    (e.dma_start(o, i, transpose=True) if tr else
                     e.dma_start(o, i, allow_slow_non_contiguous=sl)),
                    reads=reads, writes=writes, sem="d:" + q, inc=16)

    def emit_startup(self):
        d = self.dma
        for name, dst in [("identf", self.cidentf), ("identb", self.cidentb),
                          ("onesb", self.conesb), ("ones128", self.cones128),
                          ("onesAB", self.conesAB), ("tri", self.ctri)]:
            d("sync", dst[:], self.d[name].ap(), writes=("const:" + name,))
        self.vop(lambda v: v.memset(self.epsf[:], EPS), writes=("epsf",))
        for qq in range(4):
            srcq = bass.AP(self.d["x"], qq * 4 * TTS * D,
                           [[D, 128], [TTS * D, 4], [1, D]])
            dstq = bass.AP(self.Gf, qq * 4096, [[GCOLS // 2, 128], [1024, 4], [1, 1024]])
            self.dma("sync", dstq, srcq,
                     writes=(f"A{qq}",) if qq < 2 else (f"B{(qq - 2) * 2}",
                                                        f"B{(qq - 2) * 2 + 1}",))
        for tt in range(NTT):
            buf = tt * 1024
            for half in range(2):
                bank, bn = self.p[half], f"p{half}"
                for c4 in range(4):
                    c = half * 4 + c4
                    self.trans(bass.AP(self.pSt, half * 512 + c4 * 128,
                                       [[2048, 128], [1, 128]]),
                               self.gf2(buf + c * 128, 128), self.cidentf[:],
                               reads=("A0", "A1", "A2", "A3", "B0", "B1", "B2", "B3", "const:identf"), writes=(bn,),
                               sem=(c4 == 3))
                dst2 = bass.AP(self.xres, half * 4 * T + tt * TTS,
                               [[NCH * T, 128], [T, 4], [1, 128]])
                self.vop(lambda v, dd=dst2, bb=bank: v.tensor_copy(dd, bb[:]),
                         reads=(bn,), writes=(f"X{tt // 4}",))
        g = self.dma
        for n in ["a_wka", "a_wkb", "a_wva", "a_wvb", "a_wq", "a_wo", "f_wg"]:
            g("gpsimd", self.sd[n].ap(), self.d[n].ap(), writes=("scr:" + n,))
        for e in range(NE):
            for n in ("f_w1", "f_w3", "f_w2"):
                g("gpsimd", self.sd[n].ap()[e], self.d[n].ap()[e],
                  writes=(f"scr:{n}:{e}",))
        for n in ["f_w1s", "f_w3s", "f_w2s"]:
            g("gpsimd", self.sd[n].ap(), self.d[n].ap(), writes=("scr:" + n,))
        ap = bass.AP(self.G, RC0 + 64, [[GCOLS, 128], [VSTR, 16], [VH, 16]])
        self.vop(lambda v, a=ap: v.memset(a, 1.0),
                 writes=("C0", "C1", "C2", "C3"))

    def emit_ln(self, qb, dst_base, dst_stride, dst_res):
        pm, p4, p0, p1 = self.pm, self.p[4], self.p[0], self.p[1]
        xsrc = bass.AP(self.xres, qb * QB, [[NCH * T, 128], [T, NCH], [1, QB]])
        dst3 = self.g3(dst_base, dst_stride, NCH, 1, QB)
        self.sop(lambda s, dd=dst3, x=xsrc: s.copy(dd, x),
                 reads=(f"X{qb}",), writes=(dst_res,))
        xsq3 = self.g3(RE, QB, NCH, 1, QB)
        self.sop(lambda s, dd=xsq3, x=dst3: s.activation(dd, x, AF.Square),
                 reads=(dst_res,), writes=("E", "HA", "HB"))
        for c in range(NCH):
            self.mm(pm[0:1, 0:512], self.cones128[:],
                    self.g2(dst_base + c * dst_stride, QB),
                    start=(c == 0), stop=(c == NCH - 1),
                    reads=(dst_res, "const:ones128"), writes=("pm",),
                    sem=(c == NCH - 1))
        for c in range(NCH):
            self.mm(pm[32:33, 0:512], self.cones128[:], self.g2(RE + c * QB, QB),
                    start=(c == 0), stop=(c == NCH - 1),
                    reads=("E", "const:ones128"), writes=("pm",),
                    sem=(c == NCH - 1))
        rA = self.row(0, 512)
        smb = self.smb
        self.vop(lambda v: v.tensor_scalar_mul(smb[0:1, 0:512], pm[0:1, 0:512], 1.0 / D),
                 reads=("pm",), writes=("smb0",))
        self.vop(lambda v, r=rA: v.tensor_scalar_mul(r, pm[32:33, 0:512], 1.0 / D),
                 reads=("pm",), writes=("rows",))
        self.vop(lambda v: v.tensor_tensor(p4[0:1, 0:512], smb[0:1, 0:512],
                                           smb[0:1, 0:512], ALU.mult),
                 reads=("smb0",), writes=("p4",))
        self.vop(lambda v, a=rA: v.tensor_tensor(a, a, p4[0:1, 0:512], ALU.subtract),
                 reads=("rows", "p4"), writes=("rows",))
        self.sop(lambda s, a=rA: s.activation(p4[32:33, 0:512], a, AF.Sqrt,
                                              bias=self.epsf[:]),
                 reads=("rows", "epsf"), writes=("p4",))
        self.vop(lambda v, a=rA: v.reciprocal(a, p4[32:33, 0:512]),
                 reads=("p4",), writes=("rows",))
        self.vop(lambda v, a=rA: v.tensor_copy(smb[0:1, 512:1024], a),
                 reads=("rows",), writes=("smb1",))
        self.mm(p0[:], self.conesb[:], smb[0:1, 0:512], True, True,
                reads=("smb0", "const:onesb"), writes=("p0",), sem=True)
        self.mm(p1[:], self.conesb[:], smb[0:1, 512:1024], True, True,
                reads=("smb1", "const:onesb"), writes=("p1",), sem=True)
        for c in range(NCH):
            dsl = self.g2(dst_base + c * dst_stride, QB)
            self.vop(lambda v, dd=dsl: v.tensor_tensor(dd, dd, p0[:], ALU.subtract),
                     reads=(dst_res, "p0"), writes=(dst_res,), sem=False)
            self.vop(lambda v, dd=dsl: v.tensor_tensor(dd, dd, p1[:], ALU.mult),
                     reads=("p1",), writes=(dst_res,), sem=(c == NCH - 1))

    def emit_attn_a(self):
        g2, dma = self.g2, self.dma
        sd = self.sd
        for c in range(NCH):
            dma("sync", g2(RD0 + c * 256, 256),
                sd["a_wka"].ap()[0:LAT_A, c * 128:(c + 1) * 128], transpose=True,
                reads=("scr:a_wka",), writes=("D0",))
        for lc in range(2):
            dma("sync", g2(RD0 + 2048 + lc * 1024, 1024),
                sd["a_wkb"].ap()[0:D, lc * 128:(lc + 1) * 128], transpose=True,
                reads=("scr:a_wkb",), writes=("D0",))
        for c in range(NCH):
            dma("sync", g2(RD1 + c * 256, 256),
                sd["a_wva"].ap()[0:LAT_A, c * 128:(c + 1) * 128], transpose=True,
                reads=("scr:a_wva",), writes=("D1",))
        for lc in range(2):
            dma("sync", g2(RD1 + 2048 + lc * 1024, 1024),
                sd["a_wvb"].ap()[0:D, lc * 128:(lc + 1) * 128], transpose=True,
                reads=("scr:a_wvb",), writes=("D1",))
        self.emit_ln(0, RA, T, "A0")
        for qb in range(NQB):
            self._ropeA_slices(qb)
            self._lat_tok(qb, RD0, RD0 + 2048, "D0", dst="k")
            if qb + 1 < NQB:
                self.emit_ln(qb + 1, RA + (qb + 1) * QB, T, f"A{qb + 1}")
            self._lat_tok(qb, RD1, RD1 + 2048, "D1", dst="v")
        for half in range(2):
            base, wn = (RD0, "D0") if half == 0 else (RD1, "D1")
            for c in range(NCH):
                dma("sync", g2(base + c * 512, 512),
                    sd["a_wq"].ap()[half * 512:(half + 1) * 512,
                                    c * 128:(c + 1) * 128], transpose=True,
                    reads=("scr:a_wq",), writes=(wn,))
        for qb in range(NQB):
            self._ropeA_slices(qb)
            self._qtok(qb)
        self._bounds()
        for half in range(2):
            base, wn = (RD0, "D0") if half == 0 else (RD1, "D1")
            for c in range(NCH):
                dma("sync", g2(base + c * 512, 512),
                    sd["a_wo"].ap()[half * 512:(half + 1) * 512,
                                    c * 128:(c + 1) * 128], transpose=True,
                    reads=("scr:a_wo",), writes=(wn,))
        for qb in range(NQB):
            prev = None
            for h0 in range(0, NH, 2):
                sts = self._attend2(qb, h0, causal=False)
                if prev is not None:
                    self._norm_tail(prev[0])
                    self._norm_tail(prev[1])
                prev = sts
            self._norm_tail(prev[0])
            self._norm_tail(prev[1])
            self._wo(qb, bias_cols=None)

    def _ropeA_slices(self, qb):
        for name, off in (("ropeAc", 0), ("ropeAs", 512)):
            src = bass.AP(self.d[name], qb * 4 * 128 * 128,
                          [[128, 128], [128 * 128, 4], [1, 128]])
            dst = self.g3(RI + off, 128, 4, 1, 128)
            self.dma("sync", dst, src, writes=("I",))

    def _lat_tok(self, qb, wa, wb, wn, dst):
        g2 = self.g2
        for lc in range(2):
            bank, bn = self.p[2 + lc], f"p{2 + lc}"
            for c in range(NCH):
                self.mm(bank[:], g2(wa + c * 256 + lc * 128, 128),
                        g2(RA + c * T + qb * QB, QB),
                        start=(c == 0), stop=(c == NCH - 1),
                        reads=(wn, f"A{qb}"), writes=(bn,), sem=(c == NCH - 1))
            self.sop(lambda s, dd=g2(RH + lc * 512, 512), b=bank: s.copy(dd, b[:]),
                     reads=(bn,), writes=("H",))
        for ttl in range(4):
            tt = qb * 4 + ttl
            for half in range(2):
                bank, bn = self.p[half], f"p{half}"
                for lc in range(2):
                    self.mm(bank[:], g2(RH + lc * 512 + ttl * 128, 128),
                            g2(wb + lc * 1024 + half * 512, 512),
                            start=(lc == 0), stop=(lc == 1),
                            reads=("H", wn), writes=(bn,), sem=(lc == 1))
                if dst == "k":
                    self.sop(lambda s, dd=g2(RF + ttl * 1024 + half * 512, 512),
                             b=bank: s.copy(dd, b[:]), reads=(bn,), writes=("F",))
                else:
                    dap = bass.AP(self.G, RC0 + tt * VSTR + half * 8 * VH,
                                  [[GCOLS, 128], [VH, 8], [1, 64]])
                    self.sop(lambda s, dd=dap, b=bank: s.copy(dd, b[:]),
                             reads=(bn,), writes=(f"C{tt // 4}",))
        if dst == "k":
            self._ropeA_apply(qb)
            self._tok_to_feat(qb, RB, "B")

    def _qtok(self, qb):
        g2 = self.g2
        for ttl in range(4):
            for half in range(2):
                base, wn = (RD0, "D0") if half == 0 else (RD1, "D1")
                bank, bn = self.p[half], f"p{half}"
                for c in range(NCH):
                    self.mm(bank[:], g2(RA + c * T + qb * QB + ttl * 128, 128),
                            g2(base + c * 512, 512),
                            start=(c == 0), stop=(c == NCH - 1),
                            reads=(f"A{qb}", wn), writes=(bn,), sem=(c == NCH - 1))
                self.sop(lambda s, dd=g2(RF + ttl * 1024 + half * 512, 512),
                         b=bank: s.copy(dd, b[:]), reads=(bn,), writes=("F",))
        self._ropeA_apply(qb)
        self._tok_to_feat(qb, RA, "A")

    def _ropeA_apply(self, qb):
        G = self.G
        for ttl in range(4):
            qe = bass.AP(G, RF + ttl * 1024, [[GCOLS, 128], [64, 16], [2, 8]])
            qo = bass.AP(G, RF + ttl * 1024 + 1, [[GCOLS, 128], [64, 16], [2, 8]])
            ce = bass.AP(G, RI + ttl * 128, [[GCOLS, 128], [0, 16], [1, 8]])
            se = bass.AP(G, RI + 512 + ttl * 128, [[GCOLS, 128], [0, 16], [1, 8]])
            t1 = bass.AP(G, RI + 1024, [[GCOLS, 128], [8, 16], [1, 8]])
            t2 = bass.AP(G, RI + 1152, [[GCOLS, 128], [8, 16], [1, 8]])
            t3 = bass.AP(G, RI + 1280, [[GCOLS, 128], [8, 16], [1, 8]])
            t4 = bass.AP(G, RI + 1408, [[GCOLS, 128], [8, 16], [1, 8]])
            self.vop(lambda v, o=t1, a=qe, b=ce: v.tensor_tensor(o, a, b, ALU.mult),
                     reads=("F", "I"), writes=("I",), sem=False)
            self.vop(lambda v, o=t2, a=qo, b=se: v.tensor_tensor(o, a, b, ALU.mult),
                     reads=("F", "I"), writes=("I",), sem=False)
            self.vop(lambda v, o=t3, a=qe, b=se: v.tensor_tensor(o, a, b, ALU.mult),
                     reads=("F", "I"), writes=("I",), sem=False)
            self.vop(lambda v, o=t4, a=qo, b=ce: v.tensor_tensor(o, a, b, ALU.mult),
                     reads=("F", "I"), writes=("I",), sem=False)
            self.vop(lambda v, o=qe, a=t1, b=t2: v.tensor_tensor(o, a, b, ALU.subtract),
                     reads=("I",), writes=("F",), sem=False)
            self.vop(lambda v, o=qo, a=t3, b=t4: v.tensor_tensor(o, a, b, ALU.add),
                     reads=("I",), writes=("F",), sem=True)

    def _tok_to_feat(self, qb, dst_base, dreg):
        for ttl in range(4):
            tt = qb * 4 + ttl
            buf = ttl % 2
            bn = "p0" if buf == 0 else "p1"
            for c in range(NCH):
                self.trans(bass.AP(self.pStb, buf * 1024 + c * 128,
                                   [[4096, 128], [1, 128]]),
                           self.g2(RF + ttl * 1024 + c * 128, 128),
                           self.cidentb[:],
                           reads=("F", "const:identb"), writes=(bn,),
                           sem=(c == NCH - 1))
            dst = bass.AP(self.G, dst_base + tt * TTS,
                          [[GCOLS, 128], [T, NCH], [1, TTS]])
            src_ = bass.AP(self.pStb, buf * 1024, [[4096, 128], [1, 1024]])
            self.vop(lambda v, dd=dst, b=src_: v.tensor_copy(dd, b),
                     reads=(bn,), writes=(dreg + str(qb),))

    def _bounds(self):
        g2, p4 = self.g2, self.p[4]
        for reg, base, iscol in (("A", RA, True), ("B", RB, False)):
            regs = tuple(reg + str(i) for i in range(4))
            half = lambda c: (c % 2) * T
            self.sop(lambda s, dd=g2(RF + half(0), T), x=g2(base, T):
                     s.activation(dd, x, AF.Square),
                     reads=regs, writes=("F",))
            for c in range(NCH):
                if c + 1 < NCH:
                    self.sop(lambda s, dd=g2(RF + half(c + 1), T),
                             x=g2(base + (c + 1) * T, T):
                             s.activation(dd, x, AF.Square),
                             reads=regs, writes=("F",))
                for hh in range(2):
                    h = 2 * c + hh
                    for qq in range(NQB):
                        self.mm(p4[0:1, 0:512], self.conesAB[:, hh:hh + 1],
                                g2(RF + half(c) + qq * QB, QB), True, True,
                                reads=("F", "const:onesAB"), writes=("p4",),
                                sem=True)
                        col = (h * 4 + qq) if iscol else (128 + h * 4 + qq)
                        dstr = self.row(col, 1)
                        self.vop(lambda v, dd=dstr: v.tensor_reduce(
                            dd, p4[0:1, 0:512], AX.X, ALU.max),
                            reads=("p4",), writes=("rows",))
        kin = bass.AP(self.Gf, RROWF // 2 + 128, [[GCOLS // 2, 1], [4, 16], [1, 4]])
        self.vop(lambda v: v.tensor_reduce(self.row(256, 16), kin, AX.X, ALU.max),
                 reads=("rows",), writes=("rows",))
        kexp = bass.AP(self.Gf, RROWF // 2 + 256, [[GCOLS // 2, 1], [1, 16], [0, 4]])
        self.vop(lambda v: v.tensor_tensor(self.row(272, 64), self.row(0, 64),
                                           kexp, ALU.mult),
                 reads=("rows",), writes=("rows",))
        self.sop(lambda s: s.activation(self.row(336, 64), self.row(272, 64), AF.Sqrt),
                 reads=("rows",), writes=("rows",))
        self.vop(lambda v: v.tensor_scalar(self.row(272, 64), self.row(336, 64),
                                           -0.125, -0.25, ALU.mult, ALU.add),
                 reads=("rows",), writes=("rows",))
        i0 = bass.AP(self.Gf, RROWF // 2 + 272, [[GCOLS // 2, 1], [8, 8], [1, 4]])
        i1 = bass.AP(self.Gf, RROWF // 2 + 276, [[GCOLS // 2, 1], [8, 8], [1, 4]])
        self.vop(lambda v: v.tensor_tensor(self.row(336, 32), i0, i1, ALU.min),
                 reads=("rows",), writes=("rows",))
        self.vop(lambda v: v.tensor_copy(self.smb[0:1, 0:96], self.row(272, 96)),
                 reads=("rows",), writes=("smb0",))
        self.mm(self.pm[:, 0:96], self.conesb[:], self.smb[0:1, 0:96], True, True,
                reads=("smb0", "const:onesb"), writes=("pm",), sem=True)
        self.vop(lambda v: v.tensor_copy(self.negM[:], self.pm[:, 0:96]),
                 reads=("pm",), writes=("negM",))

    def _attend2(self, qb, h0, causal):
        g2, G = self.g2, self.G
        c = h0 // 2
        kmax = (4 * qb + 4) if causal else NTT
        pi = c & 1
        if pi == 0:
            obank, obn = (self.p[2], self.p[3]), ("p2", "p3")
        else:
            obank, obn = (self.p[4], self.pmv), ("p4", "pm")
        offs = (0, 64)
        bnames = (("p0", "p1"), ("s2", "s3"))
        qrhs = [G[o:o + 64, RA + c * T + qb * QB: RA + c * T + qb * QB + QB]
                for o in offs]

        def st2(kc, w):
            if kc >= kmax:
                return
            out = self.pSt[0:128, w * 1024 + (kc % 2) * 512:
                           w * 1024 + (kc % 2) * 512 + 512]
            o = offs[w]
            lhsT = G[o:o + 64, RB + c * T + kc * TTS: RB + c * T + (kc + 1) * TTS]
            self.mm(out, lhsT, qrhs[w], True, True,
                    reads=(f"A{qb}", f"B{kc // 4}"),
                    writes=(bnames[w][kc % 2],), sem=True)

        pbufs = ((RE, "HA"), (RE + 2048, "HB"))

        for w in range(2):
            st2(0, w)
            st2(1, w)
        for kcp in range(0, kmax, 2):
            pcol, pname = pbufs[(kcp // 2) % 2]
            for w in range(2):
                mcol = (h0 + w) * 4 + qb
                self.act(g2(pcol + w * 1024, 1024),
                         self.pSt[:, w * 1024:w * 1024 + 1024], AF.Exp,
                         bias=self.negM[:, mcol:mcol + 1], scale=0.125,
                         reads=bnames[w] + ("negM",), writes=(pname,))
                st2(kcp + 2, w)
                st2(kcp + 3, w)
            if causal:
                for w in range(2):
                    for j in range(2):
                        kc = kcp + j
                        if kc >= 4 * qb:
                            r = kc - 4 * qb
                            col = pcol + w * 1024 + j * 512
                            if r > 0:
                                self.vop(lambda v, a=g2(col, r * 128):
                                         v.memset(a, 0.0),
                                         reads=(pname,), writes=(pname,), sem=False)
                            self.vop(lambda v, a=g2(col + r * 128, 128):
                                     v.tensor_tensor(a, a, self.ctri[:], ALU.mult),
                                     reads=("const:tri", pname), writes=(pname,))
            for w in range(2):
                for j in range(2):
                    kc = kcp + j
                    vap = bass.AP(G, RC0 + kc * VSTR + (h0 + w) * VH,
                                  [[GCOLS, 128], [1, VH]])
                    self.mm(obank[w][0:VH, :], vap,
                            g2(pcol + w * 1024 + j * 512, 512),
                            start=(kc == 0), stop=(kc == kmax - 1),
                            reads=(f"C{kc // 4}", pname), writes=(obn[w],),
                            sem=(j == 1))
        outs = []
        for w in range(2):
            osl = G[offs[w]:offs[w] + 64, RF + c * QB: RF + c * QB + QB]
            self.vop(lambda v, o=osl, b=obank[w]: v.tensor_copy(o, b[0:64, :]),
                     reads=(obn[w],), writes=("F",))
            outs.append((qb, h0 + w, c, offs[w], osl, pi * 2 + w, pi))
        for w in range(2):
            slot = pi * 2 + w
            smbsl = self.smb[0:1, slot * 512:slot * 512 + 512]
            self.vop(lambda v, r=smbsl, b=obank[w]: v.reciprocal(r, b[64:65, :]),
                     reads=(obn[w],), writes=(f"smb{slot}",))
        return outs

    def _norm_tail(self, st):
        qb, h, c, off, osl, slot, pi = st
        w = 1 if off else 0
        smbsl = self.smb[0:1, slot * 512:slot * 512 + 512]
        if pi == 0:
            bank = self.p[2] if w == 0 else self.p[3]
            bn = "p2" if w == 0 else "p3"
            msrc = self.p23[off:off + 64, w * 512:w * 512 + 512]
        else:
            bank = self.p[4] if w == 0 else self.pmv
            bn = "p4" if w == 0 else "pm"
            msrc = (self.p4t if w == 0 else self.pm)[off:off + 64, 0:512]
        self.mm(bank[:], self.conesb[:], smbsl, True, True,
                reads=(f"smb{slot}", "const:onesb"), writes=(bn,), sem=True)
        self.vop(lambda v, o=osl, m=msrc: v.tensor_tensor(o, o, m, ALU.mult),
                 reads=(bn, "F"), writes=("F",))

    def _wo(self, qb, bias_cols):
        for oc in range(NCH):
            half, base, wn = (0, RD0, "D0") if oc < 4 else (1, RD1, "D1")
            bank, bn = self.p[oc % 2], f"p{oc % 2}"
            for ic in range(NCH):
                lhsT = self.g2(base + ic * 512 + (oc % 4) * 128, 128)
                rhs = self.g2(RF + ic * QB, QB)
                self.mm(bank[:], lhsT, rhs, start=(ic == 0), stop=(ic == NCH - 1),
                        reads=(wn, "F"), writes=(bn,), sem=(ic == NCH - 1))
            xdst = self.xres[:, oc * T + qb * QB: oc * T + qb * QB + QB]
            if bias_cols is None:
                self.vop(lambda v, xd=xdst, bk=bank: v.tensor_add(xd, xd, bk[:]),
                         reads=(bn, f"X{qb}"), writes=(f"X{qb}",))
            else:
                b = self.biasf[:, bias_cols + oc: bias_cols + oc + 1]
                self.rec.op("vector",
                            lambda v, xd=xdst, bk=bank, bb=b:
                            v.scalar_tensor_tensor(xd, bk[:], bb, xd, ALU.add, ALU.add),
                            reads=(bn, f"X{qb}", "biasf"), writes=(f"X{qb}",),
                            sem="c:vector")

    def emit_ffn(self, final=False):
        g2 = self.g2
        for qb in range(NQB):
            self.emit_ln(qb, RA + qb * QB, T, f"A{qb}")
        for c in range(NCH):
            self.dma("sync", g2(RI + 1536 + c * 16, 16),
                     self.sd["f_wg"].ap()[0:NE, c * 128:(c + 1) * 128],
                     transpose=True, reads=("scr:f_wg",), writes=("I",))
        self._gate()
        slots = [(RD0, ("D0", "D1")), (RC1, ("C2", "C3")), (RC0, ("C0", "C1"))]
        for e in (0, 1):
            self._load_w13(e, *slots[e % 3])
            self._load_w2(e, e % 4)
        for pair in range(8):
            e0, e1 = 2 * pair, 2 * pair + 1
            if pair + 1 < 8:
                self._load_w2(e0 + 2, (e0 + 2) % 4)
                self._load_w2(e1 + 2, (e1 + 2) % 4)
                self._load_w13(e0 + 2, *slots[(e0 + 2) % 3])
            for tb in range(NQB):
                self._h_e_tb(e0, tb, *slots[e0 % 3], hoff=0)
                if tb == NQB - 1 and pair + 1 < 8:
                    self._load_w13(e1 + 2, *slots[(e1 + 2) % 3])
                self._h_e_tb(e1, tb, *slots[e1 % 3], hoff=2048)
                self._w2_pair(e0, tb)
        self._load_shared()
        self._h_phase_shared(final=final)

    def _load_w13(self, e, slot, names):
        for c in range(NCH):
            self.dma("sync", self.g2(slot + c * 512, 512),
                     self.sd["f_w1"].ap()[e][0:FE, c * 128:(c + 1) * 128],
                     transpose=True, reads=(f"scr:f_w1:{e}",), writes=names)
        for c in range(NCH):
            self.dma("sync", self.g2(slot + 4096 + c * 512, 512),
                     self.sd["f_w3"].ap()[e][0:FE, c * 128:(c + 1) * 128],
                     transpose=True, reads=(f"scr:f_w3:{e}",), writes=names)

    def _load_w2(self, e, qpos):
        for fc in range(4):
            self.dma("sync", self.g2(RB + qpos * 4096 + fc * 1024, 1024),
                     self.sd["f_w2"].ap()[e][0:D, fc * 128:(fc + 1) * 128],
                     transpose=True, reads=(f"scr:f_w2:{e}",),
                     writes=(f"B{qpos}",))

    def _load_shared(self):
        for c in range(NCH):
            self.dma("sync", self.g2(RC0 + c * 1024, 1024),
                     self.sd["f_w1s"].ap()[0:D, c * 128:(c + 1) * 128],
                     transpose=True, reads=("scr:f_w1s",),
                     writes=("C0", "C1"))
        for c in range(NCH):
            self.dma("sync", self.g2(RC1 + c * 1024, 1024),
                     self.sd["f_w3s"].ap()[0:D, c * 128:(c + 1) * 128],
                     transpose=True, reads=("scr:f_w3s",), writes=("C2", "C3"))
        for c in range(NCH):
            self.dma("sync", self.g2(RB + c * 1024, 1024),
                     self.sd["f_w2s"].ap()[0:D, c * 128:(c + 1) * 128],
                     transpose=True, reads=("scr:f_w2s",),
                     writes=("B0", "B1"))

    def _gate(self):
        g2, Gf = self.g2, self.Gf
        SGF = RF // 2
        for qb in range(NQB):
            for c in range(NCH):
                self.mm(self.pm[0:16, :], g2(RI + 1536 + c * 16, 16),
                        g2(RA + c * T + qb * QB, QB),
                        start=(c == 0), stop=(c == NCH - 1),
                        reads=("I", f"A{qb}"), writes=("pm",), sem=(c == NCH - 1))
            dstap = bass.AP(Gf, SGF + qb * QB, [[GCOLS // 2, 16], [1, QB]])
            self.sop(lambda s, dd=dstap: s.activation(dd, self.pm[0:16, :], AF.Sigmoid),
                     reads=("pm",), writes=("F",))
        EF = RE // 2
        for tt in range(NTT):
            src = bass.AP(Gf, SGF + tt * TTS, [[GCOLS // 2, 16], [1, TTS]])
            self.trans(self.p[4][0:128, 0:16], src, self.cidentf[0:16, 0:16],
                       reads=("F", "const:identf"), writes=("p4",), sem=True)
            dst = bass.AP(Gf, EF + tt * 16, [[GCOLS // 2, 128], [1, 16]])
            self.vop(lambda v, dd=dst, b=self.p[4]: v.tensor_copy(dd, b[0:128, 0:16]),
                     reads=("p4",), writes=("E",))
        SG, PW, GS, TH, MX, MRow = EF, EF + 256, EF + 640, EF + 704, EF + 720, EF + 768
        WS, RC = EF + 736, EF + 752
        e3 = lambda o, s1, c1, s2, c2: bass.AP(Gf, o, [[GCOLS // 2, 128],
                                                       [s1, c1], [s2, c2]])
        pairs = [(0, 1), (0, 2), (0, 3), (1, 2), (1, 3), (2, 3)]
        for p, (a, b) in enumerate(pairs):
            self.vop(lambda v, o=e3(PW + p * 64, 4, 16, 1, 4),
                     i0=e3(SG + a, 16, 16, 4, 4), i1=e3(SG + b, 16, 16, 4, 4):
                     v.tensor_tensor(o, i0, i1, ALU.add),
                     reads=("E",), writes=("E",), sem=(p == 5))
        self.vop(lambda v: v.tensor_reduce(
            self.gf2(GS, 64), bass.AP(Gf, PW, [[GCOLS // 2, 128], [1, 64], [64, 6]]),
            AX.X, ALU.max), reads=("E",), writes=("E",))
        for p, (a, b) in enumerate(pairs):
            self.vop(lambda v, o=self.gf2(PW + p * 16, 16),
                     i0=bass.AP(Gf, GS + a, [[GCOLS // 2, 128], [4, 16]]),
                     i1=bass.AP(Gf, GS + b, [[GCOLS // 2, 128], [4, 16]]):
                     v.tensor_tensor(o, i0, i1, ALU.min),
                     reads=("E",), writes=("E",), sem=(p == 5))
        self.vop(lambda v: v.tensor_reduce(
            self.gf2(TH, 16), bass.AP(Gf, PW, [[GCOLS // 2, 128], [1, 16], [16, 6]]),
            AX.X, ALU.max), reads=("E",), writes=("E",))
        gs3 = e3(GS, 4, 16, 1, 4)
        th3 = bass.AP(Gf, TH, [[GCOLS // 2, 128], [1, 16], [0, 4]])
        self.vop(lambda v: v.tensor_tensor(e3(GS, 4, 16, 1, 4), gs3, th3, ALU.is_ge),
                 reads=("E",), writes=("E",))
        self.vop(lambda v: v.tensor_scalar_mul(self.gf2(GS, 64), self.gf2(GS, 64),
                                               256.0),
                 reads=("E",), writes=("E",))
        for r in range(4):
            self.vop(lambda v, o=e3(MRow + r * 4, 16, 16, 4, 4)
                     if False else bass.AP(Gf, MRow + r, [[GCOLS // 2, 128],
                                                          [16, 16], [4, 4]]),
                     i0=bass.AP(Gf, SG + r, [[GCOLS // 2, 128], [16, 16], [4, 4]]),
                     i1=e3(GS, 4, 16, 1, 4):
                     v.tensor_tensor(o, i0, i1, ALU.add),
                     reads=("E",), writes=("E",), sem=(r == 3))
        mred = bass.AP(Gf, MRow, [[GCOLS // 2, 128], [16, 16], [1, 16]])
        mxx = bass.AP(Gf, MX, [[GCOLS // 2, 128], [1, 16], [0, 16]])
        for j in range(4):
            self.vop(lambda v: v.tensor_reduce(self.gf2(MX, 16), mred, AX.X, ALU.max),
                     reads=("E",), writes=("E",))
            if j < 3:
                self.vop(lambda v: v.tensor_tensor(self.gf2(PW, 256),
                                                   self.gf2(MRow, 256), mxx, ALU.is_ge),
                         reads=("E",), writes=("E",))
                self.vop(lambda v: v.tensor_scalar_mul(self.gf2(PW, 256),
                                                       self.gf2(PW, 256), 1024.0),
                         reads=("E",), writes=("E",))
                self.vop(lambda v: v.tensor_tensor(self.gf2(MRow, 256),
                                                   self.gf2(MRow, 256),
                                                   self.gf2(PW, 256), ALU.subtract),
                         reads=("E",), writes=("E",))
        self.vop(lambda v: v.tensor_tensor(self.gf2(PW, 256), self.gf2(MRow, 256),
                                           mxx, ALU.is_ge),
                 reads=("E",), writes=("E",))
        self.vop(lambda v: v.scalar_tensor_tensor(self.gf2(PW, 256),
                                                  self.gf2(MRow, 256), -500.0,
                                                  self.gf2(PW, 256),
                                                  ALU.is_lt, ALU.add),
                 reads=("E",), writes=("E",))
        self.vop(lambda v: v.tensor_tensor(self.gf2(PW, 256), self.gf2(PW, 256),
                                           self.gf2(SG, 256), ALU.mult),
                 reads=("E",), writes=("E",))
        wred = bass.AP(Gf, PW, [[GCOLS // 2, 128], [16, 16], [1, 16]])
        self.vop(lambda v: v.tensor_reduce(self.gf2(WS, 16), wred, AX.X, ALU.add),
                 reads=("E",), writes=("E",))
        self.vop(lambda v: v.tensor_scalar_add(self.gf2(WS, 16), self.gf2(WS, 16), 1e-6),
                 reads=("E",), writes=("E",))
        self.vop(lambda v: v.reciprocal(self.gf2(RC, 16), self.gf2(WS, 16)),
                 reads=("E",), writes=("E",))
        rcx = bass.AP(Gf, RC, [[GCOLS // 2, 128], [1, 16], [0, 16]])
        self.vop(lambda v: v.tensor_tensor(self.gf2(MRow, 256), self.gf2(PW, 256),
                                           rcx, ALU.mult),
                 reads=("E",), writes=("E",))
        self.vop(lambda v: v.tensor_copy(self.g2(RE + 3584, 256), self.gf2(MRow, 256)),
                 reads=("E",), writes=("E",))
        for tt in range(NTT):
            src = bass.AP(self.G, RE + 3584 + tt * 16, [[GCOLS, 128], [1, 16]])
            self.trans(bass.AP(self.pStb, 0, [[4096, 16], [1, 128]]), src,
                       self.cidentb[:],
                       reads=("E", "const:identb"), writes=("p0",), sem=True)
            dst = bass.AP(self.G, RE + 1024 + tt * TTS, [[GCOLS, 16], [1, TTS]])
            self.vop(lambda v, dd=dst: v.tensor_copy(dd, bass.AP(self.pStb, 0,
                                                     [[4096, 16], [1, TTS]])),
                     reads=("p0",), writes=("E",))

    def _h_e_tb(self, e, tb, slot, names, hoff):
        g2 = self.g2
        self.mm(self.pm[:], bass.AP(self.cidentb, e, [[128, 16], [0, 128]]),
                bass.AP(self.G, RE + 1024 + tb * QB, [[GCOLS, 16], [1, QB]]),
                True, True, reads=("E", "const:identb"), writes=("pm",), sem=True)
        for fc in range(4):
            gb, gn = self.p[2 * (fc % 2)], f"p{2 * (fc % 2)}"
            ub, un = self.p[2 * (fc % 2) + 1], f"p{2 * (fc % 2) + 1}"
            for c in range(NCH):
                self.mm(gb[:], g2(slot + c * 512 + fc * 128, 128),
                        g2(RA + c * T + tb * QB, QB),
                        start=(c == 0), stop=(c == NCH - 1),
                        reads=names + (f"A{tb}",), writes=(gn,), sem=(c == NCH - 1))
            for c in range(NCH):
                self.mm(ub[:], g2(slot + 4096 + c * 512 + fc * 128, 128),
                        g2(RA + c * T + tb * QB, QB),
                        start=(c == 0), stop=(c == NCH - 1),
                        reads=names + (f"A{tb}",), writes=(un,), sem=(c == NCH - 1))
            scol = RE + (fc % 2) * 512
            self.act(g2(scol, 512), gb[:], AF.Silu, reads=(gn,), writes=("E",))
            self.vop(lambda v, o=g2(scol, 512): v.tensor_tensor(
                o, o, self.pm[:], ALU.mult),
                reads=("E", "pm"), writes=("E",), sem=False)
            hdst = g2(RF + hoff + fc * 512, 512)
            self.vop(lambda v, o=hdst, s0=g2(scol, 512), u=ub:
                     v.tensor_tensor(o, s0, u[:], ALU.mult),
                     reads=("E", un), writes=("F",))

    def _w2_pair(self, e0, tb):
        g2 = self.g2
        q0, q1 = e0 % 4, (e0 + 1) % 4
        for hh in range(NCH):
            bank, bn = (self.p[4], "p4") if hh % 2 == 0 else (self.p[2], "p2")
            n = 0
            for eq, qq in ((0, q0), (1, q1)):
                for fc in range(4):
                    self.mm(bank[:], g2(RB + qq * 4096 + fc * 1024 + hh * 128, 128),
                            g2(RF + eq * 2048 + fc * 512, 512),
                            start=(n == 0), stop=(n == 7),
                            reads=(f"B{qq}", "F"), writes=(bn,), sem=(n == 7))
                    n += 1
            xdst = self.xres[:, hh * T + tb * QB: hh * T + tb * QB + QB]
            self.vop(lambda v, xd=xdst, bk=bank: v.tensor_add(xd, xd, bk[:]),
                     reads=(bn, f"X{tb}"), writes=(f"X{tb}",))

    def _h_phase_shared(self, final=False):
        g2 = self.g2
        for tb in range(NQB):
            for fc in range(NCH):
                gb, gn = self.p[2 * (fc % 2)], f"p{2 * (fc % 2)}"
                ub, un = self.p[2 * (fc % 2) + 1], f"p{2 * (fc % 2) + 1}"
                for c in range(NCH):
                    self.mm(gb[:], g2(RC0 + c * 1024 + fc * 128, 128),
                            g2(RA + c * T + tb * QB, QB),
                            start=(c == 0), stop=(c == NCH - 1),
                            reads=("C0", "C1", f"A{tb}"), writes=(gn,),
                            sem=(c == NCH - 1))
                for c in range(NCH):
                    self.mm(ub[:], g2(RC1 + c * 1024 + fc * 128, 128),
                            g2(RA + c * T + tb * QB, QB),
                            start=(c == 0), stop=(c == NCH - 1),
                            reads=("C2", "C3", f"A{tb}"), writes=(un,),
                            sem=(c == NCH - 1))
                scol = RE + (fc % 2) * 512
                self.act(g2(scol, 512), gb[:], AF.Silu, reads=(gn,), writes=("E",))
                hdst = g2(RF + fc * 512, 512)
                self.vop(lambda v, o=hdst, s0=g2(scol, 512), u=ub:
                         v.tensor_tensor(o, s0, u[:], ALU.mult),
                         reads=("E", un), writes=("F",))
            for hh in range(NCH):
                bank, bn = self.p[4 if hh % 2 == 0 else 2], f"p{4 if hh % 2 == 0 else 2}"
                for fc in range(NCH):
                    self.mm(bank[:], g2(RB + fc * 1024 + hh * 128, 128),
                            g2(RF + fc * 512, 512),
                            start=(fc == 0), stop=(fc == NCH - 1),
                            reads=("B0", "B1", "F"), writes=(bn,),
                            sem=(fc == NCH - 1))
                xdst = self.xres[:, hh * T + tb * QB: hh * T + tb * QB + QB]
                self.vop(lambda v, xd=xdst, bk=bank: v.tensor_add(xd, xd, bk[:]),
                         reads=(bn, f"X{tb}"), writes=(f"X{tb}",))
            if final:
                for tt in range(tb * 4, tb * 4 + 4):
                    self.emit_output_tt(tt)

    def _w2_phase_shared(self):
        pass

    def emit_attn_b(self):
        g2, dma = self.g2, self.dma
        d = self.d

        def nat_load(name, dst_col, in_dim, out_dim, wn):
            nchunk = in_dim // 128
            dst = self.g3(dst_col, out_dim, nchunk, 1, out_dim)
            src = bass.AP(d[name], 0, [[out_dim, 128], [128 * out_dim, nchunk],
                                       [1, out_dim]])
            dma("gpsimd", dst, src, writes=(wn,))

        def bias_load(name, col, n):
            dst = bass.AP(self.biasf, col, [[32, 128], [32, n]])
            dst = bass.AP(self.biasf, col, [[32, 128], [1, n]])
            src = bass.AP(d[name], 0, [[1, 128], [128, n]])
            dma("gpsimd", dst, src, writes=("biasf",), slow=True)

        onesap = bass.AP(self.G, RC0 + 64, [[GCOLS, 128], [VSTR, 16], [VH, 16]])
        self.vop(lambda v, a=onesap: v.memset(a, 1.0),
                 writes=("C0", "C1", "C2", "C3"))
        nat_load("b_wkv", RD0, D, DC, "D0")
        nat_load("b_wkc", RD0 + 2048, DC, 896, "D0")
        nat_load("b_wkr", RD1, D, DR, "D1")
        self._load_rotB("b_wkr", RD1 + 1024, "D1")
        nat_load("b_wv", RD1 + 2048, DC, D, "D1")
        bias_load("b_bkv", 0, 2)
        bias_load("b_bkc", 2, 7)
        bias_load("b_bkr", 9, 1)
        self._load_rotbias("b_bkr", 10)
        self.emit_ln(0, RA, T, "A0")
        for qb in range(NQB):
            self._ropeB_slices(qb)
            self._cproj(qb, RD0, "D0", 0, "H")
            self._cc_chunks(qb, RD0 + 2048, "D0", 2, RB, "B")
            if qb + 1 < NQB:
                self.emit_ln(qb + 1, RA + (qb + 1) * QB, T, f"A{qb + 1}")
            self._rot_chunk(qb, RD1, RD1 + 1024, "D1", 9, 10, RB, "B")
            self._vproj_b(qb)
        nat_load("b_wq", RD0, D, DC, "D0")
        nat_load("b_wqc", RD0 + 2048, DC, 896, "D0")
        nat_load("b_wqr", RD1, D, DR, "D1")
        self._load_rotB("b_wqr", RD1 + 1024, "D1")
        bias_load("b_bq", 11, 2)
        bias_load("b_bqc", 13, 7)
        bias_load("b_bqr", 20, 1)
        self._load_rotbias("b_bqr", 21)
        for qb in range(NQB):
            self._ropeB_slices(qb)
            self._cproj(qb, RD0, "D0", 11, "H")
            self._rot_chunk(qb, RD1, RD1 + 1024, "D1", 20, 21, RA, "A")
            self._cc_chunks(qb, RD0 + 2048, "D0", 13, RA, "A")
        self._bounds()
        for c in range(NCH):
            dstc = (RD0 + c * 1024) if c < 4 else (RD1 + (c - 4) * 1024)
            dst = self.g2(dstc, 1024)
            src = bass.AP(d["b_wo"], c * 128 * D, [[D, 128], [1, D]])
            dma("gpsimd", dst, src, writes=("D0",) if c < 4 else ("D1",))
        bias_load("b_bo", 22, 8)
        for qb in range(NQB):
            prev = None
            for h0 in range(0, NH, 2):
                sts = self._attend2(qb, h0, causal=True)
                if prev is not None:
                    self._norm_tail(prev[0])
                    self._norm_tail(prev[1])
                prev = sts
            self._norm_tail(prev[0])
            self._norm_tail(prev[1])
            self._wo_b(qb)

    def _load_rotB(self, name, dst_col, wn):
        d1 = self.g3(dst_col, 128, NCH, 1, 64)
        s1 = bass.AP(self.d[name], 64, [[DR, 128], [128 * DR, NCH], [1, 64]])
        self.dma("gpsimd", d1, s1, writes=(wn,))
        d2 = self.g3(dst_col + 64, 128, NCH, 1, 64)
        s2 = bass.AP(self.d[name], 0, [[DR, 128], [128 * DR, NCH], [1, 64]])
        self.dma("gpsimd", d2, s2, writes=(wn,))
        self.vop(lambda v, a=d1: v.tensor_scalar_mul(a, a, -1.0),
                 reads=(wn,), writes=(wn,))

    def _load_rotbias(self, name, col):
        dsth = bass.AP(self.biasf, col, [[32, 64], [1, 1]])
        dstl = bass.AP(self.biasf, 64 * 32 + col, [[32, 64], [1, 1]])
        self.dma("gpsimd", dsth, bass.AP(self.d[name], 64, [[1, 64], [1, 1]]),
                 writes=("biasf",))
        self.dma("gpsimd", dstl, bass.AP(self.d[name], 0, [[1, 64], [1, 1]]),
                 writes=("biasf",))
        self.vop(lambda v, a=dsth: v.tensor_scalar_mul(a, a, -1.0),
                 reads=("biasf",), writes=("biasf",))

    def _ropeB_slices(self, qb):
        for name, off in (("ropeBc", 0), ("ropeBs", 512)):
            src = bass.AP(self.d[name], qb * QB, [[T, 128], [1, QB]])
            self.dma("sync", self.g2(RI + off, QB), src, writes=("I",))

    def _cproj(self, qb, wcol, wn, bcol, dreg):
        for oc in range(2):
            bank, bn = self.p[2 + oc], f"p{2 + oc}"
            for c in range(NCH):
                self.mm(bank[:], self.g2(wcol + c * DC + oc * 128, 128),
                        self.g2(RA + c * T + qb * QB, QB),
                        start=(c == 0), stop=(c == NCH - 1),
                        reads=(wn, f"A{qb}"), writes=(bn,), sem=(c == NCH - 1))
            self.act(self.g2(RH + oc * 512, 512), bank[:], AF.Gelu,
                     bias=self.biasf[:, bcol + oc: bcol + oc + 1],
                     reads=(bn, "biasf"), writes=("H",))

    def _cc_chunks(self, qb, wcol, wn, bcol, dst_base, dreg):
        for oc in range(7):
            bank, bn = self.p[oc % 2], f"p{oc % 2}"
            for lc in range(2):
                self.mm(bank[:], self.g2(wcol + lc * 896 + oc * 128, 128),
                        self.g2(RH + lc * 512, 512),
                        start=(lc == 0), stop=(lc == 1),
                        reads=(wn, "H"), writes=(bn,), sem=(lc == 1))
            self.act(self.g2(dst_base + oc * T + qb * QB, QB), bank[:], AF.Gelu,
                     bias=self.biasf[:, bcol + oc: bcol + oc + 1],
                     reads=(bn, "biasf"), writes=(dreg + str(qb),))

    def _rot_chunk(self, qb, wcol, wBcol, wn, bcol, bBcol, dst_base, dreg):
        g2 = self.g2
        for which, wc, bc, fo in ((0, wcol, bcol, 0), (1, wBcol, bBcol, 512)):
            bank, bn = self.p[which], f"p{which}"
            for c in range(NCH):
                self.mm(bank[:], g2(wc + c * 128, 128),
                        g2(RA + c * T + qb * QB, QB),
                        start=(c == 0), stop=(c == NCH - 1),
                        reads=(wn, f"A{qb}"), writes=(bn,), sem=(c == NCH - 1))
            self.act(g2(RF + fo, 512), bank[:], AF.Identity,
                     bias=self.biasf[:, bc:bc + 1],
                     reads=(bn, "biasf"), writes=("F",))
        self.vop(lambda v: v.tensor_tensor(g2(RF + 1024, 512), g2(RF, 512),
                                           g2(RI, 512), ALU.mult),
                 reads=("F", "I"), writes=("F",), sem=False)
        self.vop(lambda v: v.tensor_tensor(g2(RF + 1536, 512), g2(RF + 512, 512),
                                           g2(RI + 512, 512), ALU.mult),
                 reads=("F", "I"), writes=("F",), sem=False)
        self.vop(lambda v: v.tensor_tensor(g2(RF + 1024, 512), g2(RF + 1024, 512),
                                           g2(RF + 1536, 512), ALU.add),
                 reads=("F",), writes=("F",), sem=True)
        self.act(g2(dst_base + 7 * T + qb * QB, QB), g2(RF + 1024, 512), AF.Gelu,
                 reads=("F",), writes=(dreg + str(qb),))

    def _vproj_b(self, qb):
        g2 = self.g2
        self.dma("gpsimd", self.smb[0:1, 0:1024],
                 bass.AP(self.d["b_bv"], 0, [[D, 1], [1, D]]),
                 writes=("smb0", "smb1"))
        for ttl in range(4):
            tt = qb * 4 + ttl
            for half in range(2):
                bank, bn = self.p[half], f"p{half}"
                for lc in range(2):
                    self.mm(bank[:], g2(RH + lc * 512 + ttl * 128, 128),
                            g2(RD1 + 2048 + lc * 1024 + half * 512, 512),
                            start=(lc == 0), stop=False,
                            reads=("H", "D1"), writes=(bn,), sem=False)
                self.mm(bank[:], self.conesb[:],
                        self.smb[0:1, half * 512:(half + 1) * 512],
                        start=False, stop=True,
                        reads=(f"smb{half}", "const:onesb"), writes=(bn,), sem=True)
                dap = bass.AP(self.G, RC0 + tt * VSTR + half * 8 * VH,
                              [[GCOLS, 128], [VH, 8], [1, 64]])
                self.sop(lambda s, dd=dap, b=bank: s.copy(dd, b[:]),
                         reads=(bn,), writes=(f"C{tt // 4}",))

    def _wo_b(self, qb):
        for oc in range(NCH):
            bank, bn = self.p[oc % 2], f"p{oc % 2}"
            for ic in range(NCH):
                wc = (RD0 + ic * 1024) if ic < 4 else (RD1 + (ic - 4) * 1024)
                wn = "D0" if ic < 4 else "D1"
                self.mm(bank[:], self.g2(wc + oc * 128, 128),
                        self.g2(RF + ic * QB, QB),
                        start=(ic == 0), stop=(ic == NCH - 1),
                        reads=(wn, "F"), writes=(bn,), sem=(ic == NCH - 1))
            xdst = self.xres[:, oc * T + qb * QB: oc * T + qb * QB + QB]
            b = self.biasf[:, 22 + oc: 22 + oc + 1]
            self.rec.op("vector",
                        lambda v, xd=xdst, bk=bank, bb=b:
                        v.scalar_tensor_tensor(xd, bk[:], bb, xd, ALU.add, ALU.add),
                        reads=(bn, f"X{qb}", "biasf"), writes=(f"X{qb}",),
                        sem="c:vector")

    def emit_output_tt(self, tt):
        buf = RD0 // 2 + (tt % 2) * 1024
        for half in range(2):
            bank, bn = self.p[half], f"p{half}"
            for c4 in range(4):
                c = half * 4 + c4
                self.trans(bass.AP(self.pSt, half * 512 + c4 * 128,
                                   [[2048, 128], [1, 128]]),
                           bass.AP(self.xres, c * T + tt * TTS,
                                   [[NCH * T, 128], [1, TTS]]),
                           self.cidentf[:],
                           reads=(f"X{tt // 4}", "const:identf"), writes=(bn,),
                           sem=(c4 == 3))
            dst = bass.AP(self.Gf, buf + half * 512, [[GCOLS // 2, 128], [1, 512]])
            self.vop(lambda v, dd=dst, b=bank: v.tensor_copy(dd, b[:]),
                     reads=(bn,), writes=(f"OUT{tt % 2}",))
        srcap = self.gf2(buf, 1024)
        dstd = bass.AP(self.d["y"], tt * TTS * D, [[D, 128], [1, D]])
        self.dma("gpsimd", dstd, srcap, reads=(f"OUT{tt % 2}",), writes=())
        self.out_done.add(tt)

    def emit_output(self):
        for tt in range(NTT):
            if tt not in self.out_done:
                self.emit_output_tt(tt)
        total = self.rec.sem("d:gpsimd").n
        self.rec.wait_now("gpsimd", "d:gpsimd", total)

    def tap(self, name):
        t = self.nc.dram_tensor("tap_" + name, [128, NCH * T], F32,
                                kind="ExternalOutput")
        self.tapnames.append("tap_" + name)
        self.dma("gpsimd", t.ap(), self.xres[:], reads=("X0", "X1", "X2", "X3"), writes=())

    def tapg(self, name, col, n, reads=()):
        t = self.nc.dram_tensor("tap_" + name, [128, n], F32, kind="ExternalOutput")
        self.tapnames.append("tap_" + name)
        self.dma("gpsimd", t.ap(), self.g2(col, n), reads=tuple(reads), writes=())

    def build(self):
        self.emit_startup()
        if "attn_a" in self.stages:
            self.emit_attn_a()
            if "attn_a" in self.taps:
                self.tap("attn_a")
        if "ffn1" in self.stages:
            self.emit_ffn()
            if "ffn1" in self.taps:
                self.tap("ffn1")
        if "attn_b" in self.stages:
            self.emit_attn_b()
            if "attn_b" in self.taps:
                self.tap("attn_b")
        if "ffn2" in self.stages:
            self.emit_ffn(final=True)
        self.emit_output()
        self.rec.replay(self.nc, self.stack)
        return self.nc


def make_consts():
    c = {}
    c["identf"] = np.eye(128, dtype=np.float32)
    c["identb"] = np.eye(128, dtype=np.float32).astype(BFNP)
    c["onesb"] = np.ones((1, 128), BFNP)
    c["ones128"] = np.ones((128, 1), BFNP)
    ab = np.zeros((128, 2), np.float32)
    ab[:64, 0] = 1.0
    ab[64:, 1] = 1.0
    c["onesAB"] = ab.astype(BFNP)
    inv = (10000.0 ** (-np.arange(8) * 2.0 / 64.0)).astype(np.float64)
    angh = np.arange(16)[:, None] * inv[None, :]
    rowc = np.cos(angh).reshape(1, 128)
    rows = np.sin(angh).reshape(1, 128)
    c["ropeAc"] = np.broadcast_to(rowc, (T, 128)).astype(BFNP)
    c["ropeAs"] = np.broadcast_to(rows, (T, 128)).astype(BFNP)
    invb = (10000.0 ** (-np.arange(64) * 2.0 / 128.0)).astype(np.float64)
    angb = np.arange(T)[None, :] * invb[:, None]
    c["ropeBc"] = np.concatenate([np.cos(angb), np.cos(angb)], 0).astype(BFNP)
    c["ropeBs"] = np.concatenate([np.sin(angb), np.sin(angb)], 0).astype(BFNP)
    kk = np.arange(128)[:, None]
    qq = np.arange(128)[None, :]
    c["tri"] = (qq >= kk).astype(np.float32).astype(BFNP)
    return c


_IN_MAP_NAMES = {
    "x": None,
    "a_wq": "L0_attn_a_wq", "a_wka": "L0_attn_a_wk_a", "a_wkb": "L0_attn_a_wk_b",
    "a_wva": "L0_attn_a_wv_a", "a_wvb": "L0_attn_a_wv_b", "a_wo": "L0_attn_a_wo",
    "b_wq": "L0_attn_b_wq", "b_bq": "L0_attn_b_bq", "b_wqc": "L0_attn_b_wqc",
    "b_bqc": "L0_attn_b_bqc", "b_wqr": "L0_attn_b_wqr", "b_bqr": "L0_attn_b_bqr",
    "b_wkv": "L0_attn_b_wkv", "b_bkv": "L0_attn_b_bkv", "b_wkr": "L0_attn_b_wkr",
    "b_bkr": "L0_attn_b_bkr", "b_wkc": "L0_attn_b_wkc", "b_bkc": "L0_attn_b_bkc",
    "b_wv": "L0_attn_b_wv", "b_bv": "L0_attn_b_bv", "b_wo": "L0_attn_b_wo",
    "b_bo": "L0_attn_b_bo",
    "f_wg": "L0_ffn_w_gate", "f_w1": "L0_ffn_w1_e", "f_w3": "L0_ffn_w3_e",
    "f_w2": "L0_ffn_w2_e", "f_w1s": "L0_ffn_w1_s", "f_w3s": "L0_ffn_w3_s",
    "f_w2s": "L0_ffn_w2_s",
}


def make_in_maps(inputs, taps_consts=None):
    consts = taps_consts or make_consts()
    base = {}
    for k, refname in _IN_MAP_NAMES.items():
        if refname is not None:
            base[k] = np.ascontiguousarray(np.asarray(inputs[refname], np.float32))
    base.update(consts)
    x = np.asarray(inputs["L0_attn_a_x"], np.float32)
    in_maps = []
    for i in range(8):
        m = dict(base)
        m["x"] = np.ascontiguousarray(x[i])
        in_maps.append(m)
    return in_maps


_NC_CACHE = {}


def get_nc(stages=("attn_a", "ffn1", "attn_b", "ffn2"), taps=()):
    key = (tuple(stages), tuple(taps))
    if key not in _NC_CACHE:
        b = Build(stages=stages, taps=taps)
        _NC_CACHE[key] = (b.build(), b.tapnames)
    return _NC_CACHE[key]


def kernel(**inputs):
    nc, _ = get_nc()
    in_maps = make_in_maps(inputs)
    res = run_bass_kernel_spmd(nc, in_maps, core_ids=list(range(8)))
    return np.stack([r["y"] for r in res.results], axis=0)
